# version 24
# baseline (speedup 1.0000x reference)
from contextlib import ExitStack
import numpy as np
import ml_dtypes
import concourse.bass as bass
import concourse.mybir as mybir
from concourse.bass_utils import run_bass_kernel_spmd

F32 = mybir.dt.float32
BF16 = mybir.dt.bfloat16
I32 = mybir.dt.int32
AF = mybir.ActivationFunctionType
ALU = mybir.AluOpType
AXL = mybir.AxisListType.X
bfdt = ml_dtypes.bfloat16

D = 1024
DC = 8
T = 2304
TOWN = 2048
OWN = 2048
HALO = 256
OFF = 256
M = 4096
CTXD = 768
CC = 6
HEADS = 8
DH = 128
NE = 8
FF = 4096
FC = 32
SCALE_A = 1.5 * (DH ** -0.5)
SCALE_C = 1.0 / 512.0
EPS = 1e-5

ENGINES = ("sync", "gpsimd", "tensor", "vector", "scalar")
ENG_SEM = {e: "c_" + e for e in ENGINES}
NDMASEM = 8
DMA_SEM = {q: ["d_%s%d" % (q, i) for i in range(NDMASEM)]
           for q in ("sync", "gpsimd", "scalar")}


def tiles_of(total, step, base=0):
    out = []
    t = 0
    while t < total:
        w = min(step, total - t)
        out.append((base + t, w))
        t += w
    return out


class Tok:
    __slots__ = ("sem", "val", "eng", "dma")

    def __init__(self, sem, val, eng, dma):
        self.sem, self.val, self.eng, self.dma = sem, val, eng, dma


class Prog:

    def __init__(self, nc, stack):
        self.nc = nc
        self.sems = {}
        names = list(ENG_SEM.values())
        for lst in DMA_SEM.values():
            names += lst
        for name in names:
            self.sems[name] = stack.enter_context(nc.semaphore(name))
        self.ctr = {name: 0 for name in self.sems}
        self.qrot = {q: 0 for q in DMA_SEM}
        self.wm = {e: {} for e in ENGINES}
        self.ops = {e: [] for e in ENGINES}

    def _waits(self, eng, after):
        rec = self.ops[eng]
        for t in after:
            if t is None:
                continue
            if t.eng == eng and not t.dma:
                continue
            if self.wm[eng].get(t.sem, 0) >= t.val:
                continue
            rec.append(("w", t.sem, t.val))
            self.wm[eng][t.sem] = t.val

    def op(self, eng, fn, after=(), inc=False, force_after=()):
        rec = self.ops[eng]
        for t in force_after:
            if t is None:
                continue
            if self.wm[eng].get(t.sem, 0) >= t.val:
                continue
            rec.append(("w", t.sem, t.val))
            self.wm[eng][t.sem] = t.val
        self._waits(eng, after)
        tok = None
        sem = None
        if inc:
            sem = ENG_SEM[eng]
            self.ctr[sem] += 1
            tok = Tok(sem, self.ctr[sem], eng, False)
        self.ops[eng].append(("o", fn, sem, 1))
        return tok

    def dma(self, eng, out, in_, after=(), **kw):
        self._waits(eng, after)
        sem = DMA_SEM[eng][self.qrot[eng]]
        self.qrot[eng] = (self.qrot[eng] + 1) % NDMASEM
        self.ctr[sem] += 16
        self.ops[eng].append(
            ("o", lambda E, o=out, i=in_, k=kw: E.dma_start(out=o, in_=i, **k),
             sem, 16))
        return Tok(sem, self.ctr[sem], eng, True)

    def raw_dma(self, eng, fn, after=(), force_after=()):
        rec = self.ops[eng]
        for t in force_after:
            if t is None:
                continue
            if self.wm[eng].get(t.sem, 0) >= t.val:
                continue
            rec.append(("w", t.sem, t.val))
            self.wm[eng][t.sem] = t.val
        self._waits(eng, after)
        sem = DMA_SEM[eng][self.qrot[eng]]
        self.qrot[eng] = (self.qrot[eng] + 1) % NDMASEM
        self.ctr[sem] += 16
        self.ops[eng].append(("o", fn, sem, 16))
        return Tok(sem, self.ctr[sem], eng, True)

    def barrier(self):
        for eng in ENGINES:
            rec = self.ops[eng]
            for i in range(len(rec) - 1, -1, -1):
                if rec[i][0] == "o":
                    if rec[i][2] is None:
                        sem = ENG_SEM[eng]
                        self.ctr[sem] += 1
                        rec[i] = (rec[i][0], rec[i][1], sem, 1)
                    break
        toks = [Tok(s, v, None, True) for s, v in self.ctr.items() if v > 0]
        for eng in ENGINES:
            self._waits(eng, toks)

    def emit(self):
        if not any(self.ops[e] for e in ENGINES):
            return
        with self.nc.Block() as blk:
            for eng in ENGINES:
                if not self.ops[eng]:
                    continue
                stream = self.ops[eng]

                def runner(E, stream=stream):
                    for item in stream:
                        if item[0] == "w":
                            E.wait_ge(self.sems[item[1]], item[2])
                        else:
                            ins = item[1](E)
                            if item[2] is not None:
                                ins.then_inc(self.sems[item[2]], item[3])
                getattr(blk, eng)(runner)
        self.ops = {e: [] for e in ENGINES}


class Ring:
    def __init__(self, n):
        self.toks = [None] * n
        self.i = 0
        self.n = n

    def next(self):
        i = self.i
        self.i = (self.i + 1) % self.n
        return i, self.toks[i]

    def set(self, i, tok):
        self.toks[i] = tok


def emit_consts(nc, P, stk, dr):
    sb = lambda nm, sh, dt: stk.enter_context(nc.sbuf_tensor(nm, sh, dt)).ap()
    C = {}
    C["identb"] = sb("c_identb", [128, 128], BF16)
    C["identf"] = sb("c_identf", [128, 128], F32)
    C["ones_col_b"] = sb("c_onescb", [128, 1], BF16)
    C["ones_col_f"] = sb("c_onescf", [128, 1], F32)
    C["ones_row"] = sb("c_onesrow", [1, 128], F32)
    C["triuS"] = sb("c_triuS", [128, 128], F32)
    C["erowT"] = sb("c_erowT", [128, 144], F32)
    C["epsc"] = sb("c_epsc", [1, 1], F32)
    toks = []
    toks.append(P.dma("sync", C["identb"][:, :], dr["identb"][:, :]))
    toks.append(P.dma("sync", C["identf"][:, :], dr["identf"][:, :]))
    toks.append(P.dma("sync", C["ones_col_b"][:, :], dr["ones_col_b"][:, :]))
    toks.append(P.dma("sync", C["ones_col_f"][:, :], dr["ones_col_f"][:, :]))
    toks.append(P.dma("sync", C["ones_row"][:, :], dr["ones_row"][:, :]))
    toks.append(P.dma("sync", C["triuS"][:, :], dr["triuS"][:, :]))
    toks.append(P.dma("sync", C["erowT"][:, :], dr["erowT"][:, :]))
    toks.append(P.dma("sync", C["epsc"][:, :], dr["epsc"][:, :]))
    C["tok"] = toks
    return C


def make_ln_bufs(nc, stk, tag, wmax=512):
    sb = lambda nm, sh, dt: stk.enter_context(nc.sbuf_tensor(nm, sh, dt)).ap()
    ps = lambda nm, sh, dt: stk.enter_context(nc.psum_tensor(nm, sh, dt)).ap()
    B = {
        "sq": sb(tag + "_sq", [128, 2 * wmax], BF16),
        "st": sb(tag + "_st", [1, 4 * wmax], F32),
        "ps_sum": ps(tag + "_pssum", [128, wmax], F32),
        "ps_sq": ps(tag + "_pssq", [128, wmax], F32),
        "wmax": wmax,
        "war_sum": None, "war_sq": None,
        "war_sqbuf": [None, None],
    }
    return B


def emit_ln_gated(P, C, B, w, r_ap, dst_ap, r_toks, dst_wars, affine=None):
    W = B["wmax"]
    st = B["st"]
    m_ap, ms_ap, var_ap, rs_ap = (st[0:1, i * W:i * W + w] for i in range(4))
    iv_ap = var_ap
    mm_s = mm_q = None
    for oc in range(DC):
        r = r_ap(oc)
        slot = oc % 2
        sq = B["sq"][:, slot * W:slot * W + w]
        sqt = P.op("vector", lambda E, a=sq, b=r: E.tensor_mul(a, b, b),
                   after=[B["war_sqbuf"][slot], r_toks[oc]], inc=True)
        ones_s = C["ones_col_b"] if r.dtype == BF16 else C["ones_col_f"]
        mm_s = P.op("tensor", lambda E, b=r, o1=ones_s, oc=oc: E.matmul(
            B["ps_sum"][0:1, :w], o1[:, 0:1], b,
            start=(oc == 0), stop=(oc == DC - 1)),
            after=[r_toks[oc], C["tok"][3]] + ([B["war_sum"]] if oc == 0 else []),
            inc=(oc == DC - 1))
        mm_q = P.op("tensor", lambda E, a=sq, oc=oc: E.matmul(
            B["ps_sq"][0:1, :w], C["ones_col_b"][:, 0:1], a,
            start=(oc == 0), stop=(oc == DC - 1)),
            after=[sqt] + ([B["war_sq"]] if oc == 0 else []),
            inc=True)
        B["war_sqbuf"][slot] = mm_q
    a1 = P.op("scalar", lambda E: E.mul(m_ap, B["ps_sum"][0:1, :w], 1.0 / D),
              after=[mm_s], inc=True)
    a2 = P.op("scalar", lambda E: E.mul(ms_ap, B["ps_sq"][0:1, :w], 1.0 / D),
              after=[mm_q], inc=True)
    P.op("vector", lambda E: E.tensor_mul(var_ap, m_ap, m_ap), after=[a1])
    v2 = P.op("vector", lambda E: E.tensor_sub(var_ap, ms_ap, var_ap),
              after=[a2], inc=True)
    a3 = P.op("scalar", lambda E: E.activation(out=rs_ap, in_=var_ap,
                                               func=AF.Sqrt,
                                               bias=C["epsc"][0:1, 0:1],
                                               scale=1.0),
              after=[v2, C["tok"][7]], inc=True)
    a4 = P.op("vector", lambda E: E.reciprocal(rs_ap, rs_ap), after=[a3],
              inc=True)
    bmm = P.op("tensor", lambda E: E.matmul(
        B["ps_sum"][:, :w], C["ones_row"][0:1, :], m_ap, start=True, stop=True),
        after=[a1, C["tok"][4]], inc=True)
    brm = P.op("tensor", lambda E: E.matmul(
        B["ps_sq"][:, :w], C["ones_row"][0:1, :], rs_ap, start=True, stop=True),
        after=[a4], inc=True)
    toks = []
    for oc in range(DC):
        r = r_ap(oc)
        d = dst_ap(oc)
        P.op("vector", lambda E, b=r, d=d: E.tensor_sub(d, b, B["ps_sum"][:, :w]),
             after=[bmm, dst_wars[oc]])
        if affine is None:
            u = P.op("vector", lambda E, d=d: E.tensor_mul(d, d, B["ps_sq"][:, :w]),
                     after=[brm], inc=True)
        else:
            P.op("vector", lambda E, d=d: E.tensor_mul(d, d, B["ps_sq"][:, :w]),
                 after=[brm])
            wap, bap = affine[0](oc), affine[1](oc)
            u = P.op("vector", lambda E, d=d, wa=wap, ba=bap: E.tensor_scalar(
                d, d, wa, ba, ALU.mult, ALU.add), inc=True)
        toks.append(u)
    B["war_sum"] = B["war_sq"] = toks[-1]
    return toks


def emit_stage_a(nc, P, C, dr):
    T = TOWN
    with ExitStack() as stk:
        sb = lambda nm, sh, dt: stk.enter_context(nc.sbuf_tensor(nm, sh, dt)).ap()
        qtb = sb("a_qtb", [128, DC * T], BF16)
        aob = sb("a_aob", [128, DC * T], BF16)
        wkb = sb("a_wkb", [128, CC * 1024], BF16)
        wvb = sb("a_wvb", [128, CC * 1024], BF16)

        tok_wk = [P.dma("gpsimd", wkb[:, cc * 1024:(cc + 1) * 1024],
                        dr["wk"][cc * 128:(cc + 1) * 128, :]) for cc in range(CC)]
        tok_wv = [P.dma("gpsimd", wvb[:, cc * 1024:(cc + 1) * 1024],
                        dr["wv"][cc * 128:(cc + 1) * 128, :]) for cc in range(CC)]

        with ExitStack() as stk1:
            sb1 = lambda nm, sh, dt: stk1.enter_context(nc.sbuf_tensor(nm, sh, dt)).ap()
            ps1 = lambda nm, sh, dt: stk1.enter_context(nc.psum_tensor(nm, sh, dt)).ap()
            xbf = sb1("a_xbf", [128, DC * T], BF16)
            wqb = sb1("a_wqb", [128, DC * 1024], BF16)
            pq = [ps1("a_pq0", [128, 512], F32), ps1("a_pq1", [128, 512], F32)]
            tok_x = [P.dma("gpsimd", xbf[:, kc * T:(kc + 1) * T],
                           dr["xT"][kc * 128:(kc + 1) * 128, :]) for kc in range(DC)]
            tok_wq = [P.dma("gpsimd", wqb[:, kc * 1024:(kc + 1) * 1024],
                            dr["wq"][kc * 128:(kc + 1) * 128, :]) for kc in range(DC)]
            rng = Ring(2)
            for oc in range(DC):
                for (t0, w) in tiles_of(T, 512):
                    i, war = rng.next()
                    bank = pq[i][:, :w]
                    mm = None
                    for kc in range(DC):
                        mm = P.op("tensor", lambda E, b=bank, kc=kc, oc=oc, t0=t0, w=w: E.matmul(
                            b, wqb[:, kc * 1024 + oc * 128:kc * 1024 + (oc + 1) * 128],
                            xbf[:, kc * T + t0:kc * T + t0 + w],
                            start=(kc == 0), stop=(kc == DC - 1)),
                            after=[tok_wq[kc], tok_x[kc]] + ([war] if kc == 0 else []),
                            inc=(kc == DC - 1))
                    ev = P.op("scalar", lambda E, b=bank, oc=oc, t0=t0, w=w: E.copy(
                        qtb[:, oc * T + t0:oc * T + t0 + w], b), after=[mm], inc=True)
                    rng.set(i, ev)
            a1_done = ev
        nrm_toks = {}
        with ExitStack() as stk2:
            sb2 = lambda nm, sh, dt: stk2.enter_context(nc.sbuf_tensor(nm, sh, dt)).ap()
            ps2 = lambda nm, sh, dt: stk2.enter_context(nc.psum_tensor(nm, sh, dt)).ap()
            ktb = sb2("a_ktb", [128, M], BF16)
            vtb = sb2("a_vtb", [128, M], BF16)
            kt2 = [sb2("a_kt20", [128, M], BF16), sb2("a_kt21", [128, M], BF16)]
            vt2 = [sb2("a_vt20", [128, M], BF16), sb2("a_vt21", [128, M], BF16)]
            vsc = sb2("a_vsc", [128, 2 * 512], BF16)
            invB = sb2("a_invB", [128, 512], F32)
            inv1 = sb2("a_inv1", [1, 512], F32)
            un = sb2("a_un", [128, CC * M], BF16)
            pqk = [ps2("a_pqk0", [128, 512], F32), ps2("a_pqk1", [128, 512], F32)]
            ps_sum = ps2("a_pssum", [1, 512], F32)
            pb = ps2("a_pb", [128, 512], F32)
            pav = ps2("a_pav", [128, 512], F32)
            ptp0 = ps2("a_ptp0", [128, 128], BF16)
            ptp1 = ps2("a_ptp1", [128, 128], BF16)
            ptps = [ptp0, ptp1]
            kTd = nc.dram_tensor("a_kTd", [HEADS, 128, M], BF16)
            vTd = nc.dram_tensor("a_vTd", [HEADS, 128, M], BF16)
            tok_ctx = []
            for cc in range(CC):
                tok_ctx.append(P.dma(
                    "gpsimd", un[:, cc * M:(cc + 1) * M],
                    dr["ctxT"][cc * 128:(cc + 1) * 128, :],
                    after=[a1_done]))
            kv_st = {}
            kv_rng = Ring(2)
            last_kv_mm = None
            ktb_war = None
            vtb_war = None
            tp_toks = []
            for h in range(HEADS):
                ev_k = [None] * 8
                vsc_war = [None, None]
                for jt in range(8):
                    i, war = kv_rng.next()
                    bank = pqk[i][:, :]
                    mm = None
                    for cc in range(CC):
                        mm = P.op("tensor", lambda E, b=bank, cc=cc, h=h, jt=jt: E.matmul(
                            b, wkb[:, cc * 1024 + h * 128:cc * 1024 + (h + 1) * 128],
                            un[:, cc * M + jt * 512:cc * M + (jt + 1) * 512],
                            start=(cc == 0), stop=(cc == CC - 1)),
                            after=[tok_ctx[cc], tok_wk[cc], war] if cc == 0 else
                                  [tok_ctx[cc], tok_wk[cc]],
                            inc=(cc == CC - 1))
                    ev_k[jt] = P.op("scalar", lambda E, b=bank, jt=jt: E.copy(
                        ktb[:, jt * 512:(jt + 1) * 512], b),
                        after=[mm, ktb_war], inc=True)
                    kv_rng.set(i, ev_k[jt])
                    i2, war2 = kv_rng.next()
                    bank2 = pqk[i2][:, :]
                    mm2 = None
                    for cc in range(CC):
                        mm2 = P.op("tensor", lambda E, b=bank2, cc=cc, h=h, jt=jt: E.matmul(
                            b, wvb[:, cc * 1024 + h * 128:cc * 1024 + (h + 1) * 128],
                            un[:, cc * M + jt * 512:cc * M + (jt + 1) * 512],
                            start=(cc == 0), stop=(cc == CC - 1)),
                            after=[tok_wv[cc], war2] if cc == 0 else [tok_wv[cc]],
                            inc=(cc == CC - 1))
                    last_kv_mm = mm2
                    vslot = jt % 2
                    evv = P.op("scalar", lambda E, b=bank2, s=vslot: E.copy(
                        vsc[:, s * 512:(s + 1) * 512], b),
                        after=[mm2, vsc_war[vslot]], inc=True)
                    kv_rng.set(i2, evv)
                    last_tp_ev = None
                    for q in range(4):
                        kt = jt * 4 + q
                        pslot = kt % 2
                        tp = P.op("tensor", lambda E, s=vslot, q=q, p=pslot: E.transpose(
                            ptps[p][:, :],
                            vsc[:, s * 512 + q * 128:s * 512 + (q + 1) * 128],
                            C["identb"][:, :]),
                            after=[evv, C["tok"][0],
                                   tp_toks[-2] if len(tp_toks) >= 2 else None],
                            inc=True)
                        evt = P.op("vector", lambda E, kt=kt, p=pslot: E.tensor_copy(
                            vtb[:, kt * 128:(kt + 1) * 128],
                            ptps[p][:, :]),
                            after=[tp, vtb_war], inc=True)
                        tp_toks.append(evt)
                        last_tp_ev = evt
                    vsc_war[vslot] = last_tp_ev
                dk = P.dma("sync", kTd.ap()[h, :, :], ktb[:, :], after=[ev_k[7]])
                dv = P.dma("sync", vTd.ap()[h, :, :], vtb[:, :], after=[tp_toks[-1]])
                kv_st[h] = (dk, dv)
                ktb_war = dk
                vtb_war = dv
            buf_war = [None, None]
            kv_ld = {}
            for h in range(HEADS):
                bi = h % 2
                kv_ld[h] = (
                    P.dma("gpsimd", kt2[bi][:, :], kTd.ap()[h, :, :],
                          after=[kv_st[h][0], buf_war[bi]]),
                    P.dma("gpsimd", vt2[bi][:, :], vTd.ap()[h, :, :],
                          after=[kv_st[h][1], buf_war[bi]]),
                )
                exp_rng = [None, None]
                prev_recip = None
                prev_evb = None
                prev_nrm = None
                first_exp_extra = [last_kv_mm]
                last_head_mm = None
                for ti, (t0, w) in enumerate(tiles_of(T, 512)):
                    exp_toks = []

                    def emit_qk(kt, h=h, bi=bi, t0=t0, w=w):
                        pslot = kt % 2
                        bank = pqk[pslot][:, :w]
                        mm = P.op("tensor", lambda E, b=bank, kt=kt, bi=bi, h=h, t0=t0, w=w: E.matmul(
                            b, kt2[bi][:, kt * 128:(kt + 1) * 128],
                            qtb[:, h * T + t0:h * T + t0 + w],
                            start=True, stop=True),
                            after=[kv_ld[h][0], exp_rng[pslot]], inc=True)
                        ex = P.op("scalar", lambda E, b=bank, kt=kt, w=w: E.activation(
                            out=un[:, kt * 512:kt * 512 + w], in_=b, func=AF.Exp,
                            scale=SCALE_A),
                            after=[mm] + (first_exp_extra[0] if kt == 0 else []),
                            inc=True)
                        first_exp_extra[0] = []
                        exp_rng[pslot] = ex
                        exp_toks.append(ex)

                    def emit_sums(kt, w=w):
                        return P.op("tensor", lambda E, kt=kt, w=w: E.matmul(
                            ps_sum[0:1, :w], C["ones_col_b"][:, 0:1],
                            un[:, kt * 512:kt * 512 + w],
                            start=(kt == 0), stop=(kt == 31)),
                            after=[exp_toks[kt], C["tok"][2]] +
                                  ([prev_recip] if kt == 0 else []),
                            inc=(kt == 31))

                    def emit_av(kt, bi=bi, w=w):
                        return P.op("tensor", lambda E, kt=kt, bi=bi, w=w: E.matmul(
                            pav[:, :w], vt2[bi][:, kt * 128:(kt + 1) * 128],
                            un[:, kt * 512:kt * 512 + w],
                            start=(kt == 0), stop=(kt == 31)),
                            after=[exp_toks[kt], kv_ld[h][1]] +
                                  ([prev_nrm] if kt == 0 else []),
                            inc=(kt == 31))

                    first_exp_extra = [first_exp_extra]
                    LAG = 2
                    mm_s = mm_av = None
                    for kt in range(32 + LAG):
                        if kt < 32:
                            emit_qk(kt)
                        if kt >= LAG:
                            mm_s = emit_sums(kt - LAG)
                            mm_av = emit_av(kt - LAG)
                    first_exp_extra = first_exp_extra[0]
                    rc = P.op("vector", lambda E, w=w: E.reciprocal(
                        inv1[0:1, :w], ps_sum[0:1, :w]), after=[mm_s], inc=True)
                    prev_recip = rc
                    mb = P.op("tensor", lambda E, w=w: E.matmul(
                        pb[:, :w], C["ones_row"][0:1, :], inv1[0:1, :w],
                        start=True, stop=True),
                        after=[rc, prev_evb, C["tok"][4]], inc=True)
                    evb = P.op("scalar", lambda E, w=w: E.copy(invB[:, :w], pb[:, :w]),
                               after=[mb, prev_nrm], inc=True)
                    prev_evb = evb
                    last_head_mm = mm_av
                    nrm = P.op("vector", lambda E, h=h, t0=t0, w=w: E.tensor_tensor(
                        aob[:, h * T + t0:h * T + t0 + w], pav[:, :w], invB[:, :w],
                        ALU.mult), after=[mm_av, evb], inc=True)
                    prev_nrm = nrm
                    nrm_toks[(h, ti)] = nrm
                buf_war[bi] = last_head_mm
        P.barrier()
        with ExitStack() as stk3:
            sb3 = lambda nm, sh, dt: stk3.enter_context(nc.sbuf_tensor(nm, sh, dt)).ap()
            ps3 = lambda nm, sh, dt: stk3.enter_context(nc.psum_tensor(nm, sh, dt)).ap()
            wob = sb3("a_wob", [128, DC * 1024], BF16)
            bout = sb3("a_bout", [128, 8], F32)
            rbuf2 = [sb3("a_rbufa", [128, DC * 512], F32), sb3("a_rbufb", [128, DC * 512], F32)]
            outs2 = [sb3("a_outsa", [128, DC * 512], F32), sb3("a_outsb", [128, DC * 512], F32)]
            rx = sb3("a_rx", [128, 4 * 512], F32)
            pproj = [ps3("a_pproj0", [128, 512], F32), ps3("a_pproj1", [128, 512], F32)]
            lnb = make_ln_bufs(nc, stk3, "a_ln")
            if dr.get("dbg"):
                lnb["dbg_st"] = dr["dbg_st"][:, :]
            tok_wo = [P.dma("gpsimd", wob[:, kc * 1024:(kc + 1) * 1024],
                            dr["wo"][kc * 128:(kc + 1) * 128, :]) for kc in range(DC)]
            tok_bo = P.dma("sync", bout[:, :], dr["boutT"][:, :])
            prng = Ring(2)
            xrng = Ring(4)
            prev_store = [[None] * DC, [None] * DC]
            prev_apply = [[None] * DC, [None] * DC]
            for ti, (t0, w) in enumerate(tiles_of(T, 512)):
                par = ti % 2
                rbuf = rbuf2[par]
                outs = outs2[par]
                radd = [None] * DC
                for oc in range(DC):
                    i, war = prng.next()
                    bank = pproj[i][:, :w]
                    mm = None
                    for kc in range(DC):
                        mm = P.op("tensor", lambda E, b=bank, kc=kc, oc=oc, t0=t0, w=w: E.matmul(
                            b, wob[:, kc * 1024 + oc * 128:kc * 1024 + (oc + 1) * 128],
                            aob[:, kc * T + t0:kc * T + t0 + w],
                            start=(kc == 0), stop=(kc == DC - 1)),
                            after=[tok_wo[kc], nrm_toks[(kc, ti)]] +
                                  ([war] if kc == 0 else []),
                            inc=(kc == DC - 1))
                    ev = P.op("scalar", lambda E, b=bank, r=rbuf, oc=oc, w=w: E.activation(
                        out=r[:, oc * 512:oc * 512 + w], in_=b, func=AF.Identity,
                        bias=bout[:, oc:oc + 1], scale=1.0),
                        after=[mm, tok_bo, prev_apply[par][oc]], inc=True)
                    prng.set(i, ev)
                    xi, xwar = xrng.next()
                    xd = P.dma("sync", rx[:, xi * 512:xi * 512 + w],
                               dr["xT"][oc * 128:(oc + 1) * 128, t0:t0 + w],
                               after=[xwar])
                    ra = P.op("vector", lambda E, r=rbuf, oc=oc, xi=xi, w=w: E.tensor_add(
                        r[:, oc * 512:oc * 512 + w],
                        r[:, oc * 512:oc * 512 + w],
                        rx[:, xi * 512:xi * 512 + w]),
                        after=[ev, xd], inc=True)
                    xrng.set(xi, ra)
                    radd[oc] = ra
                r_ap = lambda oc, r=rbuf: r[:, oc * 512:oc * 512 + w]
                dst_ap = lambda oc, o=outs: o[:, oc * 512:oc * 512 + w]
                apply_toks = emit_ln_gated(P, C, lnb, w, r_ap, dst_ap, radd,
                                           prev_store[par])
                for oc in range(DC):
                    st = P.dma("sync",
                               dr["x1T"][oc * 128:(oc + 1) * 128, OFF + t0:OFF + t0 + w],
                               outs[:, oc * 512:oc * 512 + w],
                               after=[apply_toks[oc]])
                    prev_store[par][oc] = st
                    prev_apply[par][oc] = apply_toks[oc]


CAP = 768
BIGOFF = 65536.0

def emit_stage_b(nc, P, C, dr, tail_hook=None):
    T = TOWN
    NTT = T // 128
    with ExitStack() as stk:
        sb = lambda nm, sh, dt: stk.enter_context(nc.sbuf_tensor(nm, sh, dt)).ap()
        wr_sb = sb("b_wr", [128, DC * 8], BF16)
        b1s = sb("b_b1s", [128, NE * 32], F32)
        b2s = sb("b_b2s", [128, NE * 8], F32)
        x1b = sb("b_x1b", [128, DC * T], BF16)
        psm = sb("b_psm", [128, NTT], F32)
        idx = sb("b_idx", [128, NTT], I32)
        zt = sb("b_zt", [128, 1024], BF16)
        tok_wr = [P.dma("gpsimd", wr_sb[:, kc * 8:(kc + 1) * 8],
                        dr["wr"][kc * 128:(kc + 1) * 128, :]) for kc in range(DC)]
        tok_b1 = [P.dma("sync", b1s[:, e * 32:(e + 1) * 32], dr["b1moeT"][e, :, :])
                  for e in range(NE)]
        tok_b2 = [P.dma("sync", b2s[:, e * 8:(e + 1) * 8], dr["b2moeT"][e, :, :])
                  for e in range(NE)]
        tok_x1b = [P.dma("gpsimd", x1b[:, kc * T:(kc + 1) * T],
                         dr["x1T"][kc * 128:(kc + 1) * 128, OFF:OFF + T])
                   for kc in range(DC)]
        zm = P.op("vector", lambda E: E.memset(zt[:, :], 0.0), inc=True)
        ztoks = []
        for i in range(NE * CAP // 128):
            ztoks.append(P.dma("sync", dr["xg"][i * 128:(i + 1) * 128, :], zt[:, :],
                               after=[zm]))
            ztoks.append(P.dma("sync", dr["og"][i * 128:(i + 1) * 128, :], zt[:, :],
                               after=[zm]))
        with ExitStack() as stkr:
            sbr = lambda nm, sh, dt: stkr.enter_context(nc.sbuf_tensor(nm, sh, dt)).ap()
            psr = lambda nm, sh, dt: stkr.enter_context(nc.psum_tensor(nm, sh, dt)).ap()
            lg = sbr("b_lg", [8, T], F32)
            pr = psr("b_pr", [8, 512], F32)
            ev_lg = None
            prr = Ring(1)
            for (t0, w) in tiles_of(T, 512):
                mm = None
                i, war = prr.next()
                for kc in range(DC):
                    mm = P.op("tensor", lambda E, kc=kc, t0=t0, w=w: E.matmul(
                        pr[0:8, :w], wr_sb[:, kc * 8:(kc + 1) * 8],
                        x1b[:, kc * T + t0:kc * T + t0 + w],
                        start=(kc == 0), stop=(kc == DC - 1)),
                        after=[tok_wr[kc], tok_x1b[kc]] + ([war] if kc == 0 else []),
                        inc=(kc == DC - 1))
                ev_lg = P.op("scalar", lambda E, t0=t0, w=w: E.copy(
                    lg[0:8, t0:t0 + w], pr[0:8, :w]), after=[mm], inc=True)
                prr.set(i, ev_lg)
            if dr.get("dbg"):
                P.dma("sync", dr["dbg_lg"][:, :], lg[0:8, :], after=[ev_lg])
            lt_all = sbr("b_ltall", [128, NTT * 8], F32)
            dd_all = sbr("b_ddall", [128, NTT * 8], F32)
            oh_all = sbr("b_ohall", [128, NTT * 8], F32)
            eo_all = sbr("b_eoall", [128, NTT * 8], F32)
            nmx = sbr("b_nmx", [128, NTT], F32)
            cum_all = sbr("b_cumall", [128, NTT * 8], F32)
            tot_sb = sbr("b_totsb", [1, NTT * 8], F32)
            base_sb = sbr("b_basesb", [1, NTT * 8], F32)
            bb_all = sbr("b_bball", [128, NTT * 8], F32)
            rank_all = sbr("b_rank", [128, NTT], F32)
            eid_all = sbr("b_eid", [128, NTT], F32)
            slot_all = sbr("b_slot", [128, NTT], F32)
            evt_toks = []
            trng_r = Ring(2)
            ptr2 = [psr("b_ptr0", [128, 8], F32), psr("b_ptr1", [128, 8], F32)]
            for tt in range(NTT):
                ri, rwar = trng_r.next()
                tp = P.op("tensor", lambda E, tt=tt, ri=ri: E.transpose(
                    ptr2[ri][:, :], lg[0:8, tt * 128:(tt + 1) * 128],
                    C["identf"][0:8, 0:8]),
                    after=[ev_lg, C["tok"][1], rwar], inc=True)
                evt = P.op("vector", lambda E, tt=tt, ri=ri: E.tensor_copy(
                    lt_all[:, tt * 8:(tt + 1) * 8], ptr2[ri][:, :]),
                    after=[tp], inc=True)
                trng_r.set(ri, evt)
                evt_toks.append(evt)
            last_dd = None
            mx_toks = []
            for tt in range(NTT):
                mx_toks.append(P.op("vector", lambda E, tt=tt: E.reduce_max(
                    nmx[:, tt:tt + 1], lt_all[:, tt * 8:(tt + 1) * 8],
                    axis=AXL, negate=True), after=[evt_toks[tt]], inc=True))
            for tt in range(NTT):
                last_dd = P.op("vector", lambda E, tt=tt: E.tensor_scalar_add(
                    dd_all[:, tt * 8:(tt + 1) * 8],
                    lt_all[:, tt * 8:(tt + 1) * 8], nmx[:, tt:tt + 1]),
                    force_after=[mx_toks[-1]] if tt == 0 else (),
                    inc=(tt == NTT - 1))
            oh_tok = P.op("vector", lambda E: E.tensor_single_scalar(
                oh_all[:, :], dd_all[:, :], 0.0, ALU.is_ge), inc=True)
            eo_tok = P.op("scalar", lambda E: E.activation(
                out=eo_all[:, :], in_=dd_all[:, :], func=AF.Exp),
                after=[last_dd], inc=True)
            sm_tok = P.op("vector", lambda E: E.reduce_sum(
                psm.rearrange("p (t o) -> p t o", o=1),
                eo_all.rearrange("p (t o) -> p t o", o=8), axis=AXL),
                after=[eo_tok], inc=True)
            rc_tok = P.op("vector", lambda E: E.reciprocal(psm[:, :], psm[:, :]),
                          force_after=[sm_tok], inc=True)
            psm_done = rc_tok
            pc2 = [psr("b_pc0", [128, 8], F32), psr("b_pc1", [128, 8], F32)]
            ptot = psr("b_ptot", [1, NTT * 8], F32)
            crng = Ring(2)
            tot_mms = []
            for tt in range(NTT):
                ci, cwar = crng.next()
                mc = P.op("tensor", lambda E, tt=tt, ci=ci: E.matmul(
                    pc2[ci][:, :], C["triuS"][:, :], oh_all[:, tt * 8:(tt + 1) * 8],
                    start=True, stop=True),
                    after=[oh_tok, C["tok"][5], cwar], inc=True)
                ec = P.op("scalar", lambda E, tt=tt, ci=ci: E.copy(
                    cum_all[:, tt * 8:(tt + 1) * 8], pc2[ci][:, :]),
                    after=[mc], inc=True)
                crng.set(ci, ec)
                mt = P.op("tensor", lambda E, tt=tt: E.matmul(
                    ptot[0:1, tt * 8:(tt + 1) * 8], C["ones_col_f"][:, 0:1],
                    oh_all[:, tt * 8:(tt + 1) * 8], start=True, stop=True),
                    after=[oh_tok, C["tok"][3]], inc=(tt == NTT - 1))
                tot_mms.append(mt)
            et = P.op("scalar", lambda E: E.copy(tot_sb[0:1, :], ptot[0:1, :]),
                      after=[tot_mms[-1]], inc=True)
            bz = P.op("vector", lambda E: E.memset(base_sb[0:1, 0:8], 0.0), inc=True)
            prev_b = bz
            for tt in range(1, NTT):
                prev_b = P.op("vector", lambda E, tt=tt: E.tensor_add(
                    base_sb[0:1, tt * 8:(tt + 1) * 8],
                    base_sb[0:1, (tt - 1) * 8:tt * 8],
                    tot_sb[0:1, (tt - 1) * 8:tt * 8]),
                    after=[et], force_after=[prev_b], inc=True)
            pbb2 = [psr("b_pbb0", [128, 8], F32), psr("b_pbb1", [128, 8], F32)]
            brng = Ring(2)
            bb_toks = []
            for tt in range(NTT):
                bi, bwar = brng.next()
                mb = P.op("tensor", lambda E, tt=tt, bi=bi: E.matmul(
                    pbb2[bi][:, :], C["ones_row"][0:1, :],
                    base_sb[0:1, tt * 8:(tt + 1) * 8], start=True, stop=True),
                    after=[prev_b, bwar, C["tok"][4]], inc=True)
                eb = P.op("scalar", lambda E, tt=tt, bi=bi: E.copy(
                    bb_all[:, tt * 8:(tt + 1) * 8], pbb2[bi][:, :]),
                    after=[mb], inc=True)
                brng.set(bi, eb)
                bb_toks.append(eb)
            m1 = P.op("vector", lambda E: E.tensor_add(bb_all[:, :], bb_all[:, :],
                                                       cum_all[:, :]),
                      after=[bb_toks[-1], ec], inc=True)
            m2 = P.op("vector", lambda E: E.tensor_mul(bb_all[:, :], bb_all[:, :],
                                                       oh_all[:, :]), inc=True)
            m3 = P.op("vector", lambda E: E.reduce_sum(
                rank_all.rearrange("p (t o) -> p t o", o=1),
                bb_all.rearrange("p (t o) -> p t o", o=8), axis=AXL),
                inc=True)
            m4 = P.op("vector", lambda E: E.tensor_tensor(
                dd_all[:, :], oh_all[:, :], C["erowT"][:, 0:NTT * 8], ALU.mult),
                after=[C["tok"][6], eo_tok], inc=True)
            m5 = P.op("vector", lambda E: E.reduce_sum(
                eid_all.rearrange("p (t o) -> p t o", o=1),
                dd_all.rearrange("p (t o) -> p t o", o=8), axis=AXL),
                inc=True)
            m6 = P.op("vector", lambda E: E.tensor_single_scalar(
                slot_all[:, :], rank_all[:, :], float(CAP), ALU.is_ge),
                force_after=[m3], inc=True)
            m7 = P.op("vector", lambda E: E.scalar_tensor_tensor(
                slot_all[:, :], slot_all[:, :], BIGOFF, rank_all[:, :],
                ALU.mult, ALU.add), force_after=[m6], inc=True)
            m8 = P.op("vector", lambda E: E.scalar_tensor_tensor(
                slot_all[:, :], eid_all[:, :], float(CAP), slot_all[:, :],
                ALU.mult, ALU.add), force_after=[m5, m7], inc=True)
            cvt = P.op("vector", lambda E: E.tensor_copy(idx[:, :], slot_all[:, :]),
                       force_after=[m8], inc=True)
            if dr.get("dbg"):
                P.dma("sync", dr["dbg_cum"][:, :], cum_all[:, :], after=[ec])
                P.dma("sync", dr["dbg_tot"][:, :], tot_sb[0:1, :], after=[et])
                P.dma("sync", dr["dbg_base"][:, :], base_sb[0:1, :], after=[prev_b])
                P.dma("sync", dr["dbg_idx"][:, :], idx[:, :], after=[cvt])
                P.dma("sync", dr["dbg_rank"][:, :], rank_all[:, :], after=[m8])
                P.dma("sync", dr["dbg_eid"][:, :], eid_all[:, :], after=[m8])
                P.dma("sync", dr["dbg_oh"][:, :], oh_all[:, :], after=[oh_tok])
                P.dma("sync", dr["dbg_pm"][:, :], psm[:, :], after=[rc_tok])
        with ExitStack() as stks:
            sbs = lambda nm, sh, dt: stks.enter_context(nc.sbuf_tensor(nm, sh, dt)).ap()
            pss = lambda nm, sh, dt: stks.enter_context(nc.psum_tensor(nm, sh, dt)).ap()
            ttile = [sbs("b_tt0", [128, 1024], BF16), sbs("b_tt1", [128, 1024], BF16)]
            ptv = [pss("b_ptv0", [128, 128], BF16), pss("b_ptv1", [128, 128], BF16)]
            trng = Ring(2)
            srng = Ring(2)
            scat_toks = []
            for tt in range(NTT):
                si_, swar = srng.next()
                evs = []
                for kc in range(DC):
                    ti_, twar = trng.next()
                    tp = P.op("tensor", lambda E, tt=tt, kc=kc, ti_=ti_: E.transpose(
                        ptv[ti_][:, :], x1b[:, kc * T + tt * 128:kc * T + (tt + 1) * 128],
                        C["identb"][:, :]),
                        after=[tok_x1b[kc], C["tok"][0], twar] +
                              ([swar] if kc == 0 else []), inc=True)
                    ev = P.op("vector", lambda E, kc=kc, si_=si_, ti_=ti_: E.tensor_copy(
                        ttile[si_][:, kc * 128:(kc + 1) * 128], ptv[ti_][:, :]),
                        after=[tp], inc=True)
                    trng.set(ti_, ev)
                    evs.append(ev)
                sc = P.raw_dma("gpsimd", (lambda E, tt=tt, si_=si_:
                    E.indirect_dma_start(
                        dr["xg"][:, :], bass.IndirectOffsetOnAxis(idx[:, tt:tt + 1], 0),
                        ttile[si_][:, :], None,
                        bounds_check=NE * CAP, oob_is_err=False)),
                    after=evs + [cvt] + ztoks[-8:])
                srng.set(si_, sc)
                scat_toks.append(sc)
        with ExitStack() as stk2:
            sb2 = lambda nm, sh, dt: stk2.enter_context(nc.sbuf_tensor(nm, sh, dt)).ap()
            ps2 = lambda nm, sh, dt: stk2.enter_context(nc.psum_tensor(nm, sh, dt)).ap()
            xgt = [sb2("b_xgt0", [128, 1024], BF16), sb2("b_xgt1", [128, 1024], BF16)]
            otm = [sb2("b_otm0", [128, 1024], BF16), sb2("b_otm1", [128, 1024], BF16)]
            xbf = sb2("b_xbf", [128, DC * CAP], BF16)
            hT = sb2("b_hT", [128, FC * CAP], BF16)
            oT = sb2("b_oT", [128, DC * CAP], BF16)
            w1r = [sb2("b_w1r0", [128, DC * 128], BF16), sb2("b_w1r1", [128, DC * 128], BF16)]
            w2r = [sb2("b_w2r0", [128, FC * 128], BF16), sb2("b_w2r1", [128, FC * 128], BF16)]
            ptv2 = [ps2("b_ptw0", [128, 128], BF16), ps2("b_ptw1", [128, 128], BF16)]
            pe1 = [ps2("b_pe10", [128, 512], F32), ps2("b_pe11", [128, 512], F32)]
            pe2 = [ps2("b_pe20", [128, 512], F32), ps2("b_pe21", [128, 512], F32)]
            grng = Ring(2)
            orng = Ring(2)
            t2rng = Ring(2)
            w1rng = Ring(2)
            w2rng = Ring(2)
            p1rng = Ring(2)
            p2rng = Ring(2)
            og_toks = {}
            for e in range(NE):
                xin_toks = []
                for i in range(CAP // 128):
                    gi, gwar = grng.next()
                    gl = P.dma("gpsimd", xgt[gi][:, :],
                               dr["xg"][e * CAP + i * 128:e * CAP + (i + 1) * 128, :],
                               after=[gwar] + scat_toks)
                    last_ev = None
                    for kc in range(DC):
                        ti_, twar = t2rng.next()
                        tp = P.op("tensor", lambda E, gi=gi, kc=kc, ti_=ti_: E.transpose(
                            ptv2[ti_][:, :], xgt[gi][:, kc * 128:(kc + 1) * 128],
                            C["identb"][:, :]), after=[gl, twar], inc=True)
                        ev = P.op("vector", lambda E, kc=kc, i=i, ti_=ti_: E.tensor_copy(
                            xbf[:, kc * CAP + i * 128:kc * CAP + (i + 1) * 128],
                            ptv2[ti_][:, :]), after=[tp], inc=True)
                        t2rng.set(ti_, ev)
                        last_ev = ev
                        xin_toks.append(ev)
                    grng.set(gi, last_ev)
                ev_h_last = None
                for oc in range(FC):
                    wi, wwar = w1rng.next()
                    dtok = P.dma("gpsimd",
                                 w1r[wi].rearrange("p (kc n) -> p kc n", kc=DC),
                                 dr["w1moe"][e, :, oc * 128:(oc + 1) * 128]
                                 .rearrange("(kc p) n -> p kc n", p=128),
                                 after=[wwar])
                    last_mm = None
                    for (t0, w) in tiles_of(CAP, 512):
                        pi, pwar = p1rng.next()
                        bank = pe1[pi][:, :w]
                        mm = None
                        for kc in range(DC):
                            mm = P.op("tensor", lambda E, b=bank, wi=wi, kc=kc, t0=t0, w=w: E.matmul(
                                b, w1r[wi][:, kc * 128:(kc + 1) * 128],
                                xbf[:, kc * CAP + t0:kc * CAP + t0 + w],
                                start=(kc == 0), stop=(kc == DC - 1)),
                                after=[dtok, xin_toks[-1]] + ([pwar] if kc == 0 else []),
                                inc=(kc == DC - 1))
                        ev = P.op("scalar", lambda E, b=bank, e=e, oc=oc, t0=t0, w=w: E.activation(
                            out=hT[:, oc * CAP + t0:oc * CAP + t0 + w], in_=b,
                            func=AF.Gelu, bias=b1s[:, e * 32 + oc:e * 32 + oc + 1],
                            scale=1.0),
                            after=[mm, tok_b1[e]], inc=True)
                        p1rng.set(pi, ev)
                        last_mm = mm
                        ev_h_last = ev
                    w1rng.set(wi, last_mm)
                oT_toks = []
                for oc in range(DC):
                    wi, wwar = w2rng.next()
                    dtoks = []
                    for kq in range(4):
                        dtoks.append(P.dma(
                            "gpsimd", w2r[wi][:, kq * 8 * 128:(kq + 1) * 8 * 128]
                            .rearrange("p (kc n) -> p kc n", kc=8),
                            dr["w2moe"][e, kq * 1024:(kq + 1) * 1024,
                                        oc * 128:(oc + 1) * 128]
                            .rearrange("(kc p) n -> p kc n", p=128),
                            after=[wwar] if kq == 0 else []))
                    last_mm = None
                    for (t0, w) in tiles_of(CAP, 512):
                        pi, pwar = p2rng.next()
                        bank = pe2[pi][:, :w]
                        mm = None
                        for kc in range(FC):
                            mm = P.op("tensor", lambda E, b=bank, wi=wi, kc=kc, t0=t0, w=w: E.matmul(
                                b, w2r[wi][:, kc * 128:(kc + 1) * 128],
                                hT[:, kc * CAP + t0:kc * CAP + t0 + w],
                                start=(kc == 0), stop=(kc == FC - 1)),
                                after=(dtoks + [ev_h_last, pwar]) if kc == 0 else [],
                                inc=(kc == FC - 1))
                        ev = P.op("scalar", lambda E, b=bank, e=e, oc=oc, t0=t0, w=w: E.activation(
                            out=oT[:, oc * CAP + t0:oc * CAP + t0 + w], in_=b,
                            func=AF.Identity, bias=b2s[:, e * 8 + oc:e * 8 + oc + 1],
                            scale=1.0), after=[mm, tok_b2[e]], inc=True)
                        p2rng.set(pi, ev)
                        last_mm = mm
                        oT_toks.append(ev)
                    w2rng.set(wi, last_mm)
                for i in range(CAP // 128):
                    oi, owar = orng.next()
                    evs = []
                    for kc in range(DC):
                        ti_, twar = t2rng.next()
                        tp = P.op("tensor", lambda E, kc=kc, i=i, ti_=ti_: E.transpose(
                            ptv2[ti_][:, :],
                            oT[:, kc * CAP + i * 128:kc * CAP + (i + 1) * 128],
                            C["identb"][:, :]),
                            after=[oT_toks[-1]] + ([owar] if kc == 0 else []) + [twar],
                            inc=True)
                        ev = P.op("vector", lambda E, kc=kc, oi=oi, ti_=ti_: E.tensor_copy(
                            otm[oi][:, kc * 128:(kc + 1) * 128], ptv2[ti_][:, :]),
                            after=[tp], inc=True)
                        t2rng.set(ti_, ev)
                        evs.append(ev)
                    st_ = P.dma("sync", dr["og"][e * CAP + i * 128:e * CAP + (i + 1) * 128, :],
                                otm[oi][:, :], after=evs + ztoks[-8:])
                    orng.set(oi, st_)
                    og_toks[(e, i)] = st_
        with ExitStack() as stkg:
            sbg = lambda nm, sh, dt: stkg.enter_context(nc.sbuf_tensor(nm, sh, dt)).ap()
            psg = lambda nm, sh, dt: stkg.enter_context(nc.psum_tensor(nm, sh, dt)).ap()
            mtm = [sbg("b_mtm0", [128, 1024], BF16), sbg("b_mtm1", [128, 1024], BF16)]
            rbuf2 = [sbg("b_rbufa", [128, DC * 512], F32), sbg("b_rbufb", [128, DC * 512], F32)]
            outs2 = [sbg("b_outsa", [128, DC * 512], F32), sbg("b_outsb", [128, DC * 512], F32)]
            rx = sbg("b_rx", [128, 4 * 512], F32)
            ptg = [psg("b_ptg0", [128, 128], BF16), psg("b_ptg1", [128, 128], BF16)]
            lnb = make_ln_bufs(nc, stkg, "b_ln")
            mrng = Ring(2)
            tgr = Ring(2)
            xrng = Ring(4)
            prev_store = [[None] * DC, [None] * DC]
            gpar = 0
            groups = tiles_of(T, 512)
            groups = [groups[-1]] + groups[:-1]
            P.x2_tail_stores = {}
            for (g0, gw) in groups:
                par = gpar % 2
                gpar += 1
                rbuf = rbuf2[par]
                outs = outs2[par]
                radd = [None] * DC
                racc = {}
                for qi in range(gw // 128):
                    tt = (g0 + qi * 128) // 128
                    mi, mwar = mrng.next()
                    gm = P.raw_dma("gpsimd", (lambda E, tt=tt, mi=mi:
                        E.indirect_dma_start(
                            mtm[mi][:, :], None, dr["og"][:, :],
                            bass.IndirectOffsetOnAxis(idx[:, tt:tt + 1], 0),
                            bounds_check=NE * CAP, oob_is_err=False)),
                        after=[mwar, psm_done] + list(og_toks.values()))
                    sc = P.op("vector", lambda E, tt=tt, mi=mi: E.tensor_scalar_mul(
                        mtm[mi][:, :], mtm[mi][:, :], psm[:, tt:tt + 1]),
                        after=[gm], inc=True)
                    for kc in range(DC):
                        ti_, twar = tgr.next()
                        tp = P.op("tensor", lambda E, kc=kc, mi=mi, ti_=ti_: E.transpose(
                            ptg[ti_][:, :], mtm[mi][:, kc * 128:(kc + 1) * 128],
                            C["identb"][:, :]), after=[sc, twar], inc=True)
                        xi, xwar = xrng.next()
                        xd = P.dma("sync", rx[:, xi * 512:xi * 512 + 128],
                                   dr["x1T"][kc * 128:(kc + 1) * 128,
                                             OFF + g0 + qi * 128:OFF + g0 + (qi + 1) * 128],
                                   after=[xwar])
                        ra = P.op("vector", lambda E, r=rbuf, kc=kc, qi=qi, xi=xi, ti_=ti_: E.tensor_tensor(
                            r[:, kc * 512 + qi * 128:kc * 512 + (qi + 1) * 128],
                            rx[:, xi * 512:xi * 512 + 128], ptg[ti_][:, :], ALU.add),
                            after=[tp, xd, prev_store[par][kc]], inc=True)
                        tgr.set(ti_, ra)
                        xrng.set(xi, ra)
                        racc[kc] = ra
                    mrng.set(mi, ra)
                r_toks = [racc[oc] for oc in range(DC)]
                r_ap = lambda oc, r=rbuf: r[:, oc * 512:oc * 512 + gw]
                dst_ap = lambda oc, o=outs: o[:, oc * 512:oc * 512 + gw]
                ap_toks = emit_ln_gated(P, C, lnb, gw, r_ap, dst_ap, r_toks,
                                        prev_store[par])
                for oc in range(DC):
                    st_ = P.dma("sync",
                                dr["x2T"][oc * 128:(oc + 1) * 128, OFF + g0:OFF + g0 + gw],
                                outs[:, oc * 512:oc * 512 + gw], after=[ap_toks[oc]])
                    prev_store[par][oc] = st_
                    if g0 + gw == T:
                        P.x2_tail_stores[oc] = st_
                if tail_hook is not None and g0 + gw == T:
                    tail_hook()


def emit_halo_issue(nc, P, C, dr, bufs):
    si, hs = bufs
    own = dr["x2tail"]
    gat = dr["x2gat"]
    tail = getattr(P, "x2_tail_stores", {})
    sid = P.dma("sync", si[:, :], dr["halo_idx"][:, :])
    ttoks = []
    for kc in range(DC):
        ttoks.append(P.dma("sync", own.ap()[kc * 128:(kc + 1) * 128, :],
                           dr["x2T"][kc * 128:(kc + 1) * 128, T - OFF:T],
                           after=[tail.get(kc)]))
    cc = P.op("gpsimd", lambda E: E.collective_compute(
        "AllGather", ALU.bypass, [[0, 1, 2, 3, 4, 5, 6, 7]],
        [own.ap().opt()], [gat.ap().opt()]), after=ttoks, inc=True)
    hrng = Ring(2)
    for hi in range(2):
        mz = P.op("vector", lambda E, hi=hi: E.memset(hs[hi][:, :], 0.0),
                  inc=True)
        hrng.set(hi, mz)
    P.halo_state = (cc, sid, hrng)


def emit_halo_exchange(nc, P, C, dr, bufs):
    si, hs = bufs
    gat = dr["x2gat"]
    cc, sid, hrng = P.halo_state
    for kc in range(DC):
        hi, hwar = hrng.next()
        gm = P.raw_dma("gpsimd", (lambda E, kc=kc, hi=hi: E.indirect_dma_start(
            hs[hi][:, :], None, gat.ap()[:, :],
            bass.IndirectOffsetOnAxis(si[:, kc:kc + 1], 0),
            bounds_check=8 * 1024, oob_is_err=False)),
            after=[sid, hwar], force_after=[cc])
        st_ = P.dma("sync", dr["x2T"][kc * 128:(kc + 1) * 128, 0:OFF],
                    hs[hi][:, :], after=[gm])
        hrng.set(hi, st_)


def emit_stage_c(nc, P, C, dr):
    NQT = T // 128
    with ExitStack() as stk:
        sb = lambda nm, sh, dt: stk.enter_context(nc.sbuf_tensor(nm, sh, dt)).ap()
        ps = lambda nm, sh, dt: stk.enter_context(nc.psum_tensor(nm, sh, dt)).ap()
        qb = sb("c_qb", [128, DC * T], BF16)
        kb = sb("c_kb", [128, DC * T], BF16)
        vtmb = sb("c_vtmb", [128, NQT * DC * 128], BF16)
        bqkv = sb("c_bqkv", [128, 24], F32)
        lnw = sb("c_lnw", [128, 8], F32)
        lnbw = sb("c_lnbw", [128, 8], F32)
        msk = sb("c_msk", [128, 2 * 128], F32)
        hmask = sb("c_hmask", [128, 1], F32)
        tok_small = [
            P.dma("sync", bqkv[:, :], dr["bqkvT"][:, :]),
            P.dma("sync", lnw[:, :], dr["lnwT"][:, :]),
            P.dma("sync", lnbw[:, :], dr["lnbT"][:, :]),
            P.dma("sync", msk[:, 0:128], dr["mskL"][:, :]),
            P.dma("sync", msk[:, 128:256], dr["mskU"][:, :]),
            P.dma("sync", hmask[:, :], dr["hmask"][:, :]),
        ]
        ev_q = {}
        ev_k = {}
        ev_v = {}
        with ExitStack() as stk1:
            sb1 = lambda nm, sh, dt: stk1.enter_context(nc.sbuf_tensor(nm, sh, dt)).ap()
            ps1 = lambda nm, sh, dt: stk1.enter_context(nc.psum_tensor(nm, sh, dt)).ap()
            x2b = sb1("c_x2b", [128, DC * T], BF16)
            wr_ = [sb1("c_wr0", [128, DC * 128], BF16),
                   sb1("c_wr1", [128, DC * 128], BF16)]
            vsc = sb1("c_vsc", [128, 2 * 512], BF16)
            pp = [ps1("c_pp0", [128, 512], F32), ps1("c_pp1", [128, 512], F32)]
            ptv = [ps1("c_ptv0", [128, 128], BF16), ps1("c_ptv1", [128, 128], BF16)]
            tok_x2 = {}
            for (t0, w) in tiles_of(T, 512):
                for kc in range(DC):
                    tok_x2[(kc, t0)] = P.dma(
                        "gpsimd", x2b[:, kc * T + t0:kc * T + t0 + w],
                        dr["x2T"][kc * 128:(kc + 1) * 128, t0:t0 + w])
            wrng = Ring(2)
            prng = Ring(2)
            vrng = Ring(2)
            tvrng = Ring(2)
            for oc in range(24):
                wi, wwar = wrng.next()
                dtok = P.dma("gpsimd",
                             wr_[wi].rearrange("p (kc n) -> p kc n", kc=DC),
                             dr["wqkv"][:, oc * 128:(oc + 1) * 128]
                             .rearrange("(kc p) n -> p kc n", p=128),
                             after=[wwar])
                last_mm = None
                for (t0, w) in tiles_of(T, 512):
                    pi, pwar = prng.next()
                    bank = pp[pi][:, :w]
                    mm = None
                    for kc in range(DC):
                        mm = P.op("tensor", lambda E, b=bank, wi=wi, kc=kc, t0=t0, w=w: E.matmul(
                            b, wr_[wi][:, kc * 128:(kc + 1) * 128],
                            x2b[:, kc * T + t0:kc * T + t0 + w],
                            start=(kc == 0), stop=(kc == DC - 1)),
                            after=[dtok, tok_x2[(kc, t0)]] + ([pwar] if kc == 0 else []),
                            inc=(kc == DC - 1))
                    last_mm = mm
                    if oc < 8:
                        ev = P.op("scalar", lambda E, b=bank, oc=oc, t0=t0, w=w: E.activation(
                            out=qb[:, oc * T + t0:oc * T + t0 + w], in_=b,
                            func=AF.Identity, bias=bqkv[:, oc:oc + 1], scale=1.0),
                            after=[mm, tok_small[0]], inc=True)
                        ev_q[(oc, t0)] = ev
                    elif oc < 16:
                        ev = P.op("scalar", lambda E, b=bank, oc=oc, t0=t0, w=w: E.activation(
                            out=kb[:, (oc - 8) * T + t0:(oc - 8) * T + t0 + w], in_=b,
                            func=AF.Identity, bias=bqkv[:, oc:oc + 1], scale=1.0),
                            after=[mm, tok_small[0]], inc=True)
                        if t0 == 0:
                            ev = P.op("vector", lambda E, oc=oc: E.tensor_scalar_mul(
                                kb[:, (oc - 8) * T:(oc - 8) * T + OFF],
                                kb[:, (oc - 8) * T:(oc - 8) * T + OFF],
                                hmask[:, 0:1]), after=[ev, tok_small[5]], inc=True)
                        ev_k[(oc - 8, t0)] = ev
                    else:
                        vi, vwar = vrng.next()
                        vs = vsc[:, vi * 512:vi * 512 + w]
                        ev = P.op("scalar", lambda E, b=bank, s=vs, oc=oc: E.activation(
                            out=s, in_=b, func=AF.Identity,
                            bias=bqkv[:, oc:oc + 1], scale=1.0),
                            after=[mm, tok_small[0], vwar], inc=True)
                        last_evt = None
                        for q in range(w // 128):
                            kt = t0 // 128 + q
                            tvi, tvwar = tvrng.next()
                            tp = P.op("tensor", lambda E, s=vs, q=q, tvi=tvi: E.transpose(
                                ptv[tvi][:, :], s[:, q * 128:(q + 1) * 128],
                                C["identb"][:, :]),
                                after=[ev, C["tok"][0], tvwar], inc=True)
                            evt = P.op("vector", lambda E, kt=kt, oc=oc, tvi=tvi: E.tensor_copy(
                                vtmb[:, (kt * DC + (oc - 16)) * 128:
                                     (kt * DC + (oc - 16)) * 128 + 128],
                                ptv[tvi][:, :]), after=[tp], inc=True)
                            if kt < 2:
                                evt = P.op("vector", lambda E, kt=kt, oc=oc: E.tensor_scalar_mul(
                                    vtmb[:, (kt * DC + (oc - 16)) * 128:
                                         (kt * DC + (oc - 16)) * 128 + 128],
                                    vtmb[:, (kt * DC + (oc - 16)) * 128:
                                         (kt * DC + (oc - 16)) * 128 + 128],
                                    hmask[:, 0:1]), after=[evt, tok_small[5]], inc=True)
                            tvrng.set(tvi, evt)
                            ev_v[(oc - 16, kt)] = evt
                            last_evt = evt
                        vrng.set(vi, last_evt)
                    prng.set(pi, ev)
                wrng.set(wi, last_mm)
            P.barrier()
        with ExitStack() as stk2:
            sb2 = lambda nm, sh, dt: stk2.enter_context(nc.sbuf_tensor(nm, sh, dt)).ap()
            ps2 = lambda nm, sh, dt: stk2.enter_context(nc.psum_tensor(nm, sh, dt)).ap()
            sS = sb2("c_sS", [128, 3 * 128], BF16)
            bnd2 = [sb2("c_bnda", [128, DC * 512], BF16), sb2("c_bndb", [128, DC * 512], BF16)]
            rb2 = sb2("c_rb2", [128, DC * 512], F32)
            outs2 = [sb2("c_outsa", [128, DC * 512], F32), sb2("c_outsb", [128, DC * 512], F32)]
            rx = sb2("c_rx", [128, 2 * 512], F32)
            lnb = make_ln_bufs(nc, stk2, "c_ln")
            pscr = [ps2("c_pscr0", [128, 128], F32), ps2("c_pscr1", [128, 128], F32)]
            pav = [ps2("c_pav0", [128, 128], F32), ps2("c_pav1", [128, 128], F32)]
            srng = Ring(2)
            arng = Ring(2)
            sS_war = [None, None, None]
            prev_store = [[None] * DC, [None] * DC]
            prev_vadd = [None] * DC
            prev_ap1 = [None, None]
            qt_groups = [list(range(g, min(g + 4, NQT))) for g in range(0, NQT, 4)]

            def emit_attn_group(gi_, qg):
                par = gi_ % 2
                bnd = bnd2[par]
                bnd_tok = {}
                nonlocal sS_war
                for qi, qt in enumerate(qg):
                    kts = [(ki, qt - 2 + ki) for ki in range(3) if qt - 2 + ki >= 0]
                    ev_s = []
                    for (ki, kt) in kts:
                        si, swar = srng.next()
                        bank = pscr[si][:, :]
                        mm = None
                        for kc in range(DC):
                            mm = P.op("tensor", lambda E, b=bank, kc=kc, kt=kt, qt=qt: E.matmul(
                                b, kb[:, kc * T + kt * 128:kc * T + (kt + 1) * 128],
                                qb[:, kc * T + qt * 128:kc * T + (qt + 1) * 128],
                                start=(kc == 0), stop=(kc == DC - 1)),
                                after=[ev_k[(kc, (kt * 128 // 512) * 512)],
                                       ev_q[(kc, (qt * 128 // 512) * 512)]] +
                                      ([swar, sS_war[ki]] if kc == 0 else []),
                                inc=(kc == DC - 1))
                        if ki == 1:
                            evs = P.op("scalar", lambda E, b=bank, ki=ki: E.mul(
                                sS[:, ki * 128:(ki + 1) * 128], b, SCALE_C),
                                after=[mm], inc=True)
                        else:
                            mi = 0 if ki == 0 else 1
                            evs = P.op("vector", lambda E, b=bank, ki=ki, mi=mi: E.tensor_tensor(
                                sS[:, ki * 128:(ki + 1) * 128], b,
                                msk[:, mi * 128:(mi + 1) * 128], ALU.mult),
                                after=[mm, tok_small[3]], inc=True)
                        srng.set(si, evs)
                        ev_s.append((ki, kt, evs))
                    for oc in range(DC):
                        ai, awar = arng.next()
                        bank = pav[ai][:, :]
                        mm = None
                        n = len(ev_s)
                        for j, (ki, kt, evs) in enumerate(ev_s):
                            mm = P.op("tensor", lambda E, b=bank, ki=ki, kt=kt, oc=oc, j=j, n=n: E.matmul(
                                b, vtmb[:, (kt * DC + oc) * 128:(kt * DC + oc) * 128 + 128],
                                sS[:, ki * 128:(ki + 1) * 128],
                                start=(j == 0), stop=(j == n - 1)),
                                after=[evs, ev_v[(oc, kt)]] + ([awar] if j == 0 else []),
                                inc=(j == n - 1))
                        evb = P.op("scalar", lambda E, b=bank, bd=bnd, oc=oc, qi=qi: E.copy(
                            bd[:, oc * 512 + qi * 128:oc * 512 + (qi + 1) * 128], b),
                            after=[mm, prev_ap1[par]], inc=True)
                        arng.set(ai, evb)
                        bnd_tok[oc] = evb
                    sS_war = [bnd_tok[DC - 1]] * 3
                return bnd_tok

            def emit_group_ln(gi_, qg, bnd_tok):
                nonlocal prev_vadd, prev_ap1
                par = gi_ % 2
                bnd = bnd2[par]
                outs = outs2[par]
                w = 128 * len(qg)
                r_toks = [bnd_tok[oc] for oc in range(DC)]
                r_ap = lambda oc, bd=bnd: bd[:, oc * 512:oc * 512 + w]
                dst_ap = lambda oc: rb2[:, oc * 512:oc * 512 + w]
                ap1 = emit_ln_gated(P, C, lnb, w, r_ap, dst_ap, r_toks, prev_vadd,
                                    affine=(lambda oc: lnw[:, oc:oc + 1],
                                            lambda oc: lnbw[:, oc:oc + 1]))
                prev_ap1[par] = ap1[-1]
                t0g = qg[0] * 128
                vadds = [None] * DC
                xrng = Ring(2)
                for oc in range(DC):
                    xi, xwar = xrng.next()
                    xd = P.dma("sync", rx[:, xi * 512:xi * 512 + w],
                               dr["x2T"][oc * 128:(oc + 1) * 128, t0g:t0g + w],
                               after=[xwar])
                    va = P.op("vector", lambda E, oc=oc, xi=xi, w=w: E.tensor_add(
                        rb2[:, oc * 512:oc * 512 + w],
                        rb2[:, oc * 512:oc * 512 + w],
                        rx[:, xi * 512:xi * 512 + w]),
                        after=[ap1[oc], xd], inc=True)
                    xrng.set(xi, va)
                    vadds[oc] = va
                prev_vadd = vadds
                r_ap2 = lambda oc: rb2[:, oc * 512:oc * 512 + w]
                dst_ap2 = lambda oc, o=outs: o[:, oc * 512:oc * 512 + w]
                ap2 = emit_ln_gated(P, C, lnb, w, r_ap2, dst_ap2, vadds, prev_store[par])
                for oc in range(DC):
                    st_ = P.dma("sync", dr["x3T"][oc * 128:(oc + 1) * 128, t0g:t0g + w],
                                outs[:, oc * 512:oc * 512 + w], after=[ap2[oc]])
                    prev_store[par][oc] = st_

            pend = None
            for gi_, qg in enumerate(qt_groups):
                bnd_tok = emit_attn_group(gi_, qg)
                if pend is not None:
                    emit_group_ln(*pend)
                pend = (gi_, qg, bnd_tok)
            emit_group_ln(*pend)


def emit_stage_d(nc, P, C, dr):
    TH = 512
    with ExitStack() as stk:
        sb = lambda nm, sh, dt: stk.enter_context(nc.sbuf_tensor(nm, sh, dt)).ap()
        ps = lambda nm, sh, dt: stk.enter_context(nc.psum_tensor(nm, sh, dt)).ap()
        w2b = sb("d_w2b", [128, FC * 1024], BF16)
        b1s = sb("d_b1s", [128, 32], F32)
        b2s = sb("d_b2s", [128, 8], F32)
        x3b2 = [sb("d_x3ba", [128, DC * TH], BF16), sb("d_x3bb", [128, DC * TH], BF16)]
        hT = sb("d_hT", [128, FC * TH], BF16)
        chunks = tiles_of(TOWN, TH)
        w1r = [sb("d_w1r0", [128, DC * 128], BF16), sb("d_w1r1", [128, DC * 128], BF16)]
        rbuf2 = [sb("d_rbufa", [128, DC * 512], F32), sb("d_rbufb", [128, DC * 512], F32)]
        outs2 = [sb("d_outsa", [128, DC * 512], F32), sb("d_outsb", [128, DC * 512], F32)]
        rx = sb("d_rx", [128, 4 * 512], F32)
        lnb = make_ln_bufs(nc, stk, "d_ln")
        pd1 = [ps("d_pd10", [128, 512], F32), ps("d_pd11", [128, 512], F32)]
        pd2 = [ps("d_pd20", [128, 512], F32), ps("d_pd21", [128, 512], F32)]
        tok_w2 = [None] * FC
        tok_b = [P.dma("sync", b1s[:, :], dr["b1fT"][:, :]),
                 P.dma("sync", b2s[:, :], dr["b2fT"][:, :])]
        w1rng = Ring(2)
        p1rng = Ring(2)
        p2rng = Ring(2)
        prev_store = [[None] * DC, [None] * DC]
        prev_apply = [[None] * DC, [None] * DC]
        xrng = Ring(4)
        prev_third = [None, None]
        gti = 0
        for ci, (h0, THc) in enumerate(chunks):
            x3b = x3b2[ci % 2]
            tok_x3 = [P.dma("gpsimd", x3b[:, kc * TH:kc * TH + THc],
                            dr["x3T"][kc * 128:(kc + 1) * 128, OFF + h0:OFF + h0 + THc],
                            after=[prev_third[ci % 2]]) for kc in range(DC)]
            ev_h_last = None
            for oc in range(FC):
                wi, wwar = w1rng.next()
                dtok = P.dma("gpsimd",
                             w1r[wi].rearrange("p (kc n) -> p kc n", kc=DC),
                             dr["w1f"][:, oc * 128:(oc + 1) * 128]
                             .rearrange("(kc p) n -> p kc n", p=128),
                             after=[wwar])
                if ci == 0:
                    tok_w2[oc] = P.dma("gpsimd",
                                       w2b[:, oc * 1024:(oc + 1) * 1024],
                                       dr["w2f"][oc * 128:(oc + 1) * 128, :])
                last_mm = None
                for (t0, w) in tiles_of(THc, 512):
                    pi, pwar = p1rng.next()
                    bank = pd1[pi][:, :w]
                    mm = None
                    for kc in range(DC):
                        mm = P.op("tensor", lambda E, b=bank, x3b=x3b, wi=wi, kc=kc, t0=t0, w=w: E.matmul(
                            b, w1r[wi][:, kc * 128:(kc + 1) * 128],
                            x3b[:, kc * TH + t0:kc * TH + t0 + w],
                            start=(kc == 0), stop=(kc == DC - 1)),
                            after=[dtok, tok_x3[kc]] + ([pwar] if kc == 0 else []),
                            inc=(kc == DC - 1))
                    ev = P.op("scalar", lambda E, b=bank, oc=oc, t0=t0, w=w: E.activation(
                        out=hT[:, oc * TH + t0:oc * TH + t0 + w], in_=b,
                        func=AF.Relu, bias=b1s[:, oc:oc + 1], scale=1.0),
                        after=[mm, tok_b[0]], inc=True)
                    p1rng.set(pi, ev)
                    last_mm = mm
                    ev_h_last = ev
                w1rng.set(wi, last_mm)
            for (t0, w) in tiles_of(THc, 512):
                par = gti % 2
                gti += 1
                rbuf = rbuf2[par]
                outs = outs2[par]
                radd = [None] * DC
                for oc in range(DC):
                    pi, pwar = p2rng.next()
                    bank = pd2[pi][:, :w]
                    mm = None
                    for kc in range(FC):
                        mm = P.op("tensor", lambda E, b=bank, kc=kc, oc=oc, t0=t0, w=w: E.matmul(
                            b, w2b[:, kc * 1024 + oc * 128:kc * 1024 + (oc + 1) * 128],
                            hT[:, kc * TH + t0:kc * TH + t0 + w],
                            start=(kc == 0), stop=(kc == FC - 1)),
                            after=[tok_w2[kc], ev_h_last] + ([pwar] if kc == 0 else []),
                            inc=(kc == FC - 1))
                    ev = P.op("scalar", lambda E, b=bank, r=rbuf, oc=oc, w=w: E.activation(
                        out=r[:, oc * 512:oc * 512 + w], in_=b, func=AF.Identity,
                        bias=b2s[:, oc:oc + 1], scale=1.0),
                        after=[mm, tok_b[1], prev_apply[par][oc]], inc=True)
                    p2rng.set(pi, ev)
                    xi, xwar = xrng.next()
                    xd = P.dma("sync", rx[:, xi * 512:xi * 512 + w],
                               dr["x3T"][oc * 128:(oc + 1) * 128,
                                         OFF + h0 + t0:OFF + h0 + t0 + w],
                               after=[xwar])
                    ra = P.op("vector", lambda E, r=rbuf, oc=oc, xi=xi, w=w: E.tensor_add(
                        r[:, oc * 512:oc * 512 + w],
                        r[:, oc * 512:oc * 512 + w],
                        rx[:, xi * 512:xi * 512 + w]),
                        after=[ev, xd], inc=True)
                    xrng.set(xi, ra)
                    radd[oc] = ra
                r_ap = lambda oc, r=rbuf: r[:, oc * 512:oc * 512 + w]
                dst_ap = lambda oc, o=outs: o[:, oc * 512:oc * 512 + w]
                ap_toks = emit_ln_gated(P, C, lnb, w, r_ap, dst_ap, radd,
                                        prev_store[par])
                for oc in range(DC):
                    st_ = P.dma("sync", dr["yT"][oc * 128:(oc + 1) * 128, h0 + t0:h0 + t0 + w],
                                outs[:, oc * 512:oc * 512 + w], after=[ap_toks[oc]])
                    prev_store[par][oc] = st_
                    prev_apply[par][oc] = ap_toks[oc]
            prev_third[ci % 2] = ev_h_last
            ev_h_last = None


def _dram_inputs_a(nc):
    dr = {}
    dr["xT"] = nc.dram_tensor("xT", [D, TOWN], F32, kind="ExternalInput").ap()
    dr["ctxT"] = nc.dram_tensor("ctxT", [CTXD, M], F32, kind="ExternalInput").ap()
    dr["wq"] = nc.dram_tensor("wq", [D, D], F32, kind="ExternalInput").ap()
    dr["wk"] = nc.dram_tensor("wk", [CTXD, D], F32, kind="ExternalInput").ap()
    dr["wv"] = nc.dram_tensor("wv", [CTXD, D], F32, kind="ExternalInput").ap()
    dr["wo"] = nc.dram_tensor("wo", [D, D], F32, kind="ExternalInput").ap()
    dr["boutT"] = nc.dram_tensor("boutT", [128, 8], F32, kind="ExternalInput").ap()
    return dr


def _dram_inputs_b(nc):
    dr = {}
    dr["wr"] = nc.dram_tensor("wr", [D, 8], F32, kind="ExternalInput").ap()
    dr["w1moe"] = nc.dram_tensor("w1moe", [NE, D, FF], F32, kind="ExternalInput").ap()
    dr["b1moeT"] = nc.dram_tensor("b1moeT", [NE, 128, 32], F32, kind="ExternalInput").ap()
    dr["w2moe"] = nc.dram_tensor("w2moe", [NE, FF, D], F32, kind="ExternalInput").ap()
    dr["b2moeT"] = nc.dram_tensor("b2moeT", [NE, 128, 8], F32, kind="ExternalInput").ap()
    return dr


def _dram_inputs_c(nc):
    dr = {}
    dr["wqkv"] = nc.dram_tensor("wqkv", [D, 3 * D], F32, kind="ExternalInput").ap()
    dr["bqkvT"] = nc.dram_tensor("bqkvT", [128, 24], F32, kind="ExternalInput").ap()
    dr["lnwT"] = nc.dram_tensor("lnwT", [128, 8], F32, kind="ExternalInput").ap()
    dr["lnbT"] = nc.dram_tensor("lnbT", [128, 8], F32, kind="ExternalInput").ap()
    dr["mskL"] = nc.dram_tensor("mskL", [128, 128], F32, kind="ExternalInput").ap()
    dr["mskU"] = nc.dram_tensor("mskU", [128, 128], F32, kind="ExternalInput").ap()
    return dr


def _dram_inputs_d(nc):
    dr = {}
    dr["w1f"] = nc.dram_tensor("w1f", [D, FF], F32, kind="ExternalInput").ap()
    dr["b1fT"] = nc.dram_tensor("b1fT", [128, 32], F32, kind="ExternalInput").ap()
    dr["w2f"] = nc.dram_tensor("w2f", [FF, D], F32, kind="ExternalInput").ap()
    dr["b2fT"] = nc.dram_tensor("b2fT", [128, 8], F32, kind="ExternalInput").ap()
    return dr


def _dram_consts(nc):
    dr = {}
    dr["identb"] = nc.dram_tensor("identb", [128, 128], BF16, kind="ExternalInput").ap()
    dr["identf"] = nc.dram_tensor("identf", [128, 128], F32, kind="ExternalInput").ap()
    dr["ones_col_b"] = nc.dram_tensor("ones_col_b", [128, 1], BF16, kind="ExternalInput").ap()
    dr["ones_col_f"] = nc.dram_tensor("ones_col_f", [128, 1], F32, kind="ExternalInput").ap()
    dr["ones_row"] = nc.dram_tensor("ones_row", [1, 128], F32, kind="ExternalInput").ap()
    dr["triuS"] = nc.dram_tensor("triuS", [128, 128], F32, kind="ExternalInput").ap()
    dr["erowT"] = nc.dram_tensor("erowT", [128, 144], F32, kind="ExternalInput").ap()
    dr["epsc"] = nc.dram_tensor("epsc", [1, 1], F32, kind="ExternalInput").ap()
    return dr


def build_stage(stage, debug=False):
    nc = bass.Bass("TRN2", target_bir_lowering=False)
    with ExitStack() as stack:
        P = Prog(nc, stack)
        dr = _dram_consts(nc)
        if stage == "a":
            dr.update(_dram_inputs_a(nc))
            dr["x1T"] = nc.dram_tensor("x1T", [D, T], F32, kind="ExternalOutput").ap()
            if debug:
                dr["dbg"] = True
                dr["dbg_qt"] = nc.dram_tensor("dbg_qt", [128, DC * T], BF16, kind="ExternalOutput").ap()
                dr["dbg_ao"] = nc.dram_tensor("dbg_ao", [128, DC * T], BF16, kind="ExternalOutput").ap()
                dr["dbg_kt"] = nc.dram_tensor("dbg_kt", [128, M], BF16, kind="ExternalOutput").ap()
                dr["dbg_vt"] = nc.dram_tensor("dbg_vt", [128, M], BF16, kind="ExternalOutput").ap()
                dr["dbg_p"] = nc.dram_tensor("dbg_p", [128, 16384], BF16, kind="ExternalOutput").ap()
                dr["dbg_invB"] = nc.dram_tensor("dbg_invB", [128, 512], F32, kind="ExternalOutput").ap()
                dr["dbg_un0"] = nc.dram_tensor("dbg_un0", [128, 16384], BF16, kind="ExternalOutput").ap()
                dr["dbg_inv1"] = nc.dram_tensor("dbg_inv1", [1, 512], F32, kind="ExternalOutput").ap()
                dr["dbg_r"] = nc.dram_tensor("dbg_r", [128, DC * 512], F32, kind="ExternalOutput").ap()
                dr["dbg_st"] = nc.dram_tensor("dbg_st", [1, 6 * 512], F32, kind="ExternalOutput").ap()
        elif stage == "b":
            dr.update(_dram_inputs_b(nc))
            dr["x1T"] = nc.dram_tensor("x1T", [D, T], F32, kind="ExternalInput").ap()
            dr["x2T"] = nc.dram_tensor("x2T", [D, T], F32, kind="ExternalOutput").ap()
            if not debug:
                dr["xg"] = nc.dram_tensor("xg", [NE * CAP, D], BF16).ap()
                dr["og"] = nc.dram_tensor("og", [NE * CAP, D], BF16).ap()
            if debug:
                dr["dbg"] = True
                dr["dbg_lg"] = nc.dram_tensor("dbg_lg", [8, T], F32, kind="ExternalOutput").ap()
                for nm, sh in [("dbg_oh", [128, T // 128 * 8]), ("dbg_pm", [128, T // 128]),
                               ("dbg_rank", [128, T // 128]), ("dbg_eid", [128, T // 128])]:
                    dr[nm] = nc.dram_tensor(nm, sh, F32, kind="ExternalOutput").ap()
                dr["dbg_idx"] = nc.dram_tensor("dbg_idx", [128, T // 128], I32, kind="ExternalOutput").ap()
                dr["dbg_cum"] = nc.dram_tensor("dbg_cum", [128, T // 128 * 8], F32, kind="ExternalOutput").ap()
                dr["dbg_tot"] = nc.dram_tensor("dbg_tot", [1, T // 128 * 8], F32, kind="ExternalOutput").ap()
                dr["dbg_base"] = nc.dram_tensor("dbg_base", [1, T // 128 * 8], F32, kind="ExternalOutput").ap()
                dr["xg"] = nc.dram_tensor("xg", [NE * CAP, D], BF16, kind="ExternalOutput").ap()
                dr["og"] = nc.dram_tensor("og", [NE * CAP, D], BF16, kind="ExternalOutput").ap()
        elif stage == "c":
            dr.update(_dram_inputs_c(nc))
            dr["x2T"] = nc.dram_tensor("x2T", [D, T], F32, kind="ExternalInput").ap()
            dr["x3T"] = nc.dram_tensor("x3T", [D, T], F32, kind="ExternalOutput").ap()
            dr["hmask"] = nc.dram_tensor("hmask", [128, 1], F32, kind="ExternalInput").ap()
        elif stage == "d":
            dr.update(_dram_inputs_d(nc))
            dr["x3T"] = nc.dram_tensor("x3T", [D, T], F32, kind="ExternalInput").ap()
            dr["yT"] = nc.dram_tensor("yT", [D, TOWN], F32, kind="ExternalOutput").ap()
        with ExitStack() as cstk:
            Cs = emit_consts(nc, P, cstk, dr)
            if stage == "a":
                emit_stage_a(nc, P, Cs, dr)
            elif stage == "b":
                emit_stage_b(nc, P, Cs, dr)
            elif stage == "c":
                emit_stage_c(nc, P, Cs, dr)
            elif stage == "d":
                emit_stage_d(nc, P, Cs, dr)
            P.barrier()
            P.emit()
    return nc


def build_stage_a(debug=False):
    return build_stage("a", debug)


def build_fused():
    nc = bass.Bass("TRN2", target_bir_lowering=False)
    with ExitStack() as stack:
        P = Prog(nc, stack)
        dr = _dram_consts(nc)
        dr.update(_dram_inputs_a(nc))
        dr.update(_dram_inputs_b(nc))
        dr.update(_dram_inputs_c(nc))
        dr.update(_dram_inputs_d(nc))
        dr["x1T"] = nc.dram_tensor("x1T", [D, T], F32).ap()
        dr["x2T"] = nc.dram_tensor("x2T", [D, T], F32).ap()
        dr["x3T"] = nc.dram_tensor("x3T", [D, T], F32).ap()
        dr["yT"] = nc.dram_tensor("yT", [D, TOWN], F32, kind="ExternalOutput").ap()
        dr["xg"] = nc.dram_tensor("xg", [NE * CAP, D], BF16).ap()
        dr["og"] = nc.dram_tensor("og", [NE * CAP, D], BF16).ap()
        dr["x2tail"] = nc.dram_tensor("x2tail", [D, OFF], F32)
        dr["x2gat"] = nc.dram_tensor("x2gat", [8 * D, OFF], F32,
                                     addr_space="Shared")
        dr["halo_idx"] = nc.dram_tensor("halo_idx", [128, 8], I32, kind="ExternalInput").ap()
        dr["hmask"] = nc.dram_tensor("hmask", [128, 1], F32, kind="ExternalInput").ap()
        with ExitStack() as cstk:
            Cs = emit_consts(nc, P, cstk, dr)
            xsi = cstk.enter_context(nc.sbuf_tensor("x_si", [128, 8], I32)).ap()
            xhs = [cstk.enter_context(nc.sbuf_tensor("x_hs0", [128, OFF], F32)).ap(),
                   cstk.enter_context(nc.sbuf_tensor("x_hs1", [128, OFF], F32)).ap()]
            emit_stage_a(nc, P, Cs, dr)
            P.barrier()
            P.emit()
            emit_stage_b(nc, P, Cs, dr,
                         tail_hook=lambda: emit_halo_issue(nc, P, Cs, dr,
                                                           (xsi, xhs)))
            emit_halo_exchange(nc, P, Cs, dr, (xsi, xhs))
            P.barrier()
            P.emit()
            emit_stage_c(nc, P, Cs, dr)
            P.barrier()
            P.emit()
            emit_stage_d(nc, P, Cs, dr)
            P.barrier()
            P.emit()
    return nc


def host_consts():
    return {
        "identb": np.eye(128, dtype=bfdt),
        "identf": np.eye(128, dtype=np.float32),
        "ones_col_b": np.ones((128, 1), dtype=bfdt),
        "ones_col_f": np.ones((128, 1), dtype=np.float32),
        "ones_row": np.ones((1, 128), dtype=np.float32),
        "triuS": np.triu(np.ones((128, 128), np.float32), 1),
        "erowT": np.tile(np.arange(8, dtype=np.float32), (128, T // 128)),
        "epsc": np.full((1, 1), EPS, np.float32),
    }


def core_window(c):
    b, h = c // 2, c % 2
    start = 0 if h == 0 else 4096 - T
    own0 = 0 if h == 0 else T - OWN
    return b, start, own0


def slice_stream(full, c):
    b, h = c // 2, c % 2
    return np.ascontiguousarray(full[b, h * OWN:(h + 1) * OWN, :].T.astype(np.float32))


def assemble_stream(parts):
    out = np.empty((4, 4096, D), np.float32)
    for c in range(8):
        b, h = c // 2, c % 2
        out[b, h * OWN:(h + 1) * OWN, :] = parts[c].T
    return out


def halo_inputs(c):
    h = c % 2
    if h == 1:
        idx = ((c - 1) * 1024 + np.arange(8)[None, :] * 128 +
               np.arange(128)[:, None]).astype(np.int32)
        hm = np.ones((128, 1), np.float32)
    else:
        idx = np.full((128, 8), 10 ** 6, np.int32)
        hm = np.zeros((128, 1), np.float32)
    return idx, hm


def _bT(v, chunks):
    return np.ascontiguousarray(np.asarray(v, np.float32).reshape(chunks, 128).T)


def stage_weight_maps(inputs):
    f32 = np.float32
    a = {
        "ctxT": np.ascontiguousarray(
            np.asarray(inputs["L0_attn_context"], f32).reshape(M, CTXD).T),
        "wq": np.asarray(inputs["L0_attn_Wq"], f32),
        "wk": np.asarray(inputs["L0_attn_Wk"], f32),
        "wv": np.asarray(inputs["L0_attn_Wv"], f32),
        "wo": np.asarray(inputs["L0_attn_Wout"], f32),
        "boutT": _bT(inputs["L0_attn_bout"], 8),
    }
    b = {
        "wr": np.asarray(inputs["L0_ffn_Wr"], f32),
        "w1moe": np.asarray(inputs["L0_ffn_W1"], f32),
        "b1moeT": np.ascontiguousarray(
            np.asarray(inputs["L0_ffn_b1"], f32).reshape(NE, 32, 128).transpose(0, 2, 1)),
        "w2moe": np.asarray(inputs["L0_ffn_W2"], f32),
        "b2moeT": np.ascontiguousarray(
            np.asarray(inputs["L0_ffn_b2"], f32).reshape(NE, 8, 128).transpose(0, 2, 1)),
    }
    ones = np.ones((128, 128), f32)
    cc = {
        "wqkv": np.asarray(inputs["L1_attn_W_qkv"], f32),
        "bqkvT": _bT(inputs["L1_attn_b_qkv"], 24),
        "lnwT": _bT(inputs["L1_attn_ln_w"], 8),
        "lnbT": _bT(inputs["L1_attn_ln_b"], 8),
        "mskL": np.ascontiguousarray(np.tril(ones, -1) * SCALE_C),
        "mskU": np.ascontiguousarray(np.triu(ones) * SCALE_C),
    }
    d = {
        "w1f": np.asarray(inputs["L1_ffn_W1"], f32),
        "b1fT": _bT(inputs["L1_ffn_b1"], 32),
        "w2f": np.asarray(inputs["L1_ffn_W2"], f32),
        "b2fT": _bT(inputs["L1_ffn_b2"], 8),
    }
    return a, b, cc, d


def stage_a_in_maps(inputs):
    cst = host_consts()
    a, _, _, _ = stage_weight_maps(inputs)
    base = dict(cst); base.update(a)
    x = np.asarray(inputs["L0_attn_x"], np.float32)
    maps = []
    for c in range(8):
        m = dict(base)
        m["xT"] = slice_stream(x, c)
        maps.append(m)
    return maps


def _launch(stage, base, stream_key, stream_full):
    nc = build_stage(stage)
    maps = []
    for c in range(8):
        m = dict(base)
        m[stream_key] = slice_stream(stream_full, c)
        maps.append(m)
    res = run_bass_kernel_spmd(nc, maps, core_ids=list(range(8)))
    return res


def kernel(**inputs):
    cst = host_consts()
    wa, wb, wc, wd = stage_weight_maps(inputs)
    x = np.asarray(inputs["L0_attn_x"], np.float32)
    nc = build_fused()
    base = dict(cst)
    for d_ in (wa, wb, wc, wd):
        base.update(d_)
    maps = []
    for c in range(8):
        m = dict(base)
        m["xT"] = slice_stream(x, c)
        m["halo_idx"], m["hmask"] = halo_inputs(c)
        maps.append(m)
    res = run_bass_kernel_spmd(nc, maps, core_ids=list(range(8)))
    return assemble_stream([res.results[c]["yT"] for c in range(8)])


def kernel_unfused(**inputs):
    cst = host_consts()
    wa, wb, wc, wd = stage_weight_maps(inputs)
    x = np.asarray(inputs["L0_attn_x"], np.float32)
    base_a = dict(cst); base_a.update(wa)
    res = _launch("a", base_a, "xT", x)
    x1 = assemble_stream([res.results[c]["x1T"] for c in range(8)])
    base_b = dict(cst); base_b.update(wb)
    res = _launch("b", base_b, "x1T", x1)
    x2 = assemble_stream([res.results[c]["x2T"] for c in range(8)])
    base_c = dict(cst); base_c.update(wc)
    res = _launch("c", base_c, "x2T", x2)
    x3 = assemble_stream([res.results[c]["x3T"] for c in range(8)])
    base_d = dict(cst); base_d.update(wd)
    res = _launch("d", base_d, "x3T", x3)
    return assemble_stream([res.results[c]["yT"] for c in range(8)])



# version 25 (final)
# speedup vs baseline: 1.0232x; 1.0232x over previous
from contextlib import ExitStack
import numpy as np
import ml_dtypes
import concourse.bass as bass
import concourse.mybir as mybir
from concourse.bass_utils import run_bass_kernel_spmd

F32 = mybir.dt.float32
BF16 = mybir.dt.bfloat16
I32 = mybir.dt.int32
AF = mybir.ActivationFunctionType
ALU = mybir.AluOpType
AXL = mybir.AxisListType.X
bfdt = ml_dtypes.bfloat16

D = 1024
DC = 8
T = 2304
TOWN = 2048
OWN = 2048
HALO = 256
OFF = 256
M = 4096
CTXD = 768
CC = 6
HEADS = 8
DH = 128
NE = 8
FF = 4096
FC = 32
SCALE_A = 1.5 * (DH ** -0.5)
SCALE_C = 1.0 / 512.0
EPS = 1e-5

ENGINES = ("sync", "gpsimd", "tensor", "vector", "scalar")
ENG_SEM = {e: "c_" + e for e in ENGINES}
NDMASEM = 8
DMA_SEM = {q: ["d_%s%d" % (q, i) for i in range(NDMASEM)]
           for q in ("sync", "gpsimd", "scalar")}


def tiles_of(total, step, base=0):
    out = []
    t = 0
    while t < total:
        w = min(step, total - t)
        out.append((base + t, w))
        t += w
    return out


class Tok:
    __slots__ = ("sem", "val", "eng", "dma")

    def __init__(self, sem, val, eng, dma):
        self.sem, self.val, self.eng, self.dma = sem, val, eng, dma


class Prog:

    def __init__(self, nc, stack):
        self.nc = nc
        self.sems = {}
        names = list(ENG_SEM.values())
        for lst in DMA_SEM.values():
            names += lst
        for name in names:
            self.sems[name] = stack.enter_context(nc.semaphore(name))
        self.ctr = {name: 0 for name in self.sems}
        self.qrot = {q: 0 for q in DMA_SEM}
        self.wm = {e: {} for e in ENGINES}
        self.ops = {e: [] for e in ENGINES}

    def _waits(self, eng, after):
        rec = self.ops[eng]
        for t in after:
            if t is None:
                continue
            if t.eng == eng and not t.dma:
                continue
            if self.wm[eng].get(t.sem, 0) >= t.val:
                continue
            rec.append(("w", t.sem, t.val))
            self.wm[eng][t.sem] = t.val

    def op(self, eng, fn, after=(), inc=False, force_after=()):
        rec = self.ops[eng]
        for t in force_after:
            if t is None:
                continue
            if self.wm[eng].get(t.sem, 0) >= t.val:
                continue
            rec.append(("w", t.sem, t.val))
            self.wm[eng][t.sem] = t.val
        self._waits(eng, after)
        tok = None
        sem = None
        if inc:
            sem = ENG_SEM[eng]
            self.ctr[sem] += 1
            tok = Tok(sem, self.ctr[sem], eng, False)
        self.ops[eng].append(("o", fn, sem, 1))
        return tok

    def dma(self, eng, out, in_, after=(), **kw):
        self._waits(eng, after)
        sem = DMA_SEM[eng][self.qrot[eng]]
        self.qrot[eng] = (self.qrot[eng] + 1) % NDMASEM
        self.ctr[sem] += 16
        self.ops[eng].append(
            ("o", lambda E, o=out, i=in_, k=kw: E.dma_start(out=o, in_=i, **k),
             sem, 16))
        return Tok(sem, self.ctr[sem], eng, True)

    def raw_dma(self, eng, fn, after=(), force_after=()):
        rec = self.ops[eng]
        for t in force_after:
            if t is None:
                continue
            if self.wm[eng].get(t.sem, 0) >= t.val:
                continue
            rec.append(("w", t.sem, t.val))
            self.wm[eng][t.sem] = t.val
        self._waits(eng, after)
        sem = DMA_SEM[eng][self.qrot[eng]]
        self.qrot[eng] = (self.qrot[eng] + 1) % NDMASEM
        self.ctr[sem] += 16
        self.ops[eng].append(("o", fn, sem, 16))
        return Tok(sem, self.ctr[sem], eng, True)

    def barrier(self):
        for eng in ENGINES:
            rec = self.ops[eng]
            for i in range(len(rec) - 1, -1, -1):
                if rec[i][0] == "o":
                    if rec[i][2] is None:
                        sem = ENG_SEM[eng]
                        self.ctr[sem] += 1
                        rec[i] = (rec[i][0], rec[i][1], sem, 1)
                    break
        toks = [Tok(s, v, None, True) for s, v in self.ctr.items() if v > 0]
        for eng in ENGINES:
            self._waits(eng, toks)

    def emit(self):
        if not any(self.ops[e] for e in ENGINES):
            return
        with self.nc.Block() as blk:
            for eng in ENGINES:
                if not self.ops[eng]:
                    continue
                stream = self.ops[eng]

                def runner(E, stream=stream):
                    for item in stream:
                        if item[0] == "w":
                            E.wait_ge(self.sems[item[1]], item[2])
                        else:
                            ins = item[1](E)
                            if item[2] is not None:
                                ins.then_inc(self.sems[item[2]], item[3])
                getattr(blk, eng)(runner)
        self.ops = {e: [] for e in ENGINES}


class Ring:
    def __init__(self, n):
        self.toks = [None] * n
        self.i = 0
        self.n = n

    def next(self):
        i = self.i
        self.i = (self.i + 1) % self.n
        return i, self.toks[i]

    def set(self, i, tok):
        self.toks[i] = tok


def emit_consts(nc, P, stk, dr):
    sb = lambda nm, sh, dt: stk.enter_context(nc.sbuf_tensor(nm, sh, dt)).ap()
    C = {}
    C["identb"] = sb("c_identb", [128, 128], BF16)
    C["identf"] = sb("c_identf", [128, 128], F32)
    C["ones_col_b"] = sb("c_onescb", [128, 1], BF16)
    C["ones_col_f"] = sb("c_onescf", [128, 1], F32)
    C["ones_row"] = sb("c_onesrow", [1, 128], F32)
    C["triuS"] = sb("c_triuS", [128, 128], F32)
    C["erowT"] = sb("c_erowT", [128, 144], F32)
    C["epsc"] = sb("c_epsc", [1, 1], F32)
    toks = []
    toks.append(P.dma("sync", C["identb"][:, :], dr["identb"][:, :]))
    toks.append(P.dma("sync", C["identf"][:, :], dr["identf"][:, :]))
    toks.append(P.dma("sync", C["ones_col_b"][:, :], dr["ones_col_b"][:, :]))
    toks.append(P.dma("sync", C["ones_col_f"][:, :], dr["ones_col_f"][:, :]))
    toks.append(P.dma("sync", C["ones_row"][:, :], dr["ones_row"][:, :]))
    toks.append(P.dma("sync", C["triuS"][:, :], dr["triuS"][:, :]))
    toks.append(P.dma("sync", C["erowT"][:, :], dr["erowT"][:, :]))
    toks.append(P.dma("sync", C["epsc"][:, :], dr["epsc"][:, :]))
    C["tok"] = toks
    return C


def make_ln_bufs(nc, stk, tag, wmax=512):
    sb = lambda nm, sh, dt: stk.enter_context(nc.sbuf_tensor(nm, sh, dt)).ap()
    ps = lambda nm, sh, dt: stk.enter_context(nc.psum_tensor(nm, sh, dt)).ap()
    B = {
        "sq": sb(tag + "_sq", [128, 2 * wmax], BF16),
        "st": sb(tag + "_st", [1, 4 * wmax], F32),
        "ps_sum": ps(tag + "_pssum", [128, wmax], F32),
        "ps_sq": ps(tag + "_pssq", [128, wmax], F32),
        "wmax": wmax,
        "war_sum": None, "war_sq": None,
        "war_sqbuf": [None, None],
    }
    return B


def emit_ln_gated(P, C, B, w, r_ap, dst_ap, r_toks, dst_wars, affine=None):
    W = B["wmax"]
    st = B["st"]
    m_ap, ms_ap, var_ap, rs_ap = (st[0:1, i * W:i * W + w] for i in range(4))
    iv_ap = var_ap
    mm_s = mm_q = None
    for oc in range(DC):
        r = r_ap(oc)
        slot = oc % 2
        sq = B["sq"][:, slot * W:slot * W + w]
        sqt = P.op("vector", lambda E, a=sq, b=r: E.tensor_mul(a, b, b),
                   after=[B["war_sqbuf"][slot], r_toks[oc]], inc=True)
        ones_s = C["ones_col_b"] if r.dtype == BF16 else C["ones_col_f"]
        mm_s = P.op("tensor", lambda E, b=r, o1=ones_s, oc=oc: E.matmul(
            B["ps_sum"][0:1, :w], o1[:, 0:1], b,
            start=(oc == 0), stop=(oc == DC - 1)),
            after=[r_toks[oc], C["tok"][3]] + ([B["war_sum"]] if oc == 0 else []),
            inc=(oc == DC - 1))
        mm_q = P.op("tensor", lambda E, a=sq, oc=oc: E.matmul(
            B["ps_sq"][0:1, :w], C["ones_col_b"][:, 0:1], a,
            start=(oc == 0), stop=(oc == DC - 1)),
            after=[sqt] + ([B["war_sq"]] if oc == 0 else []),
            inc=True)
        B["war_sqbuf"][slot] = mm_q
    a1 = P.op("scalar", lambda E: E.mul(m_ap, B["ps_sum"][0:1, :w], 1.0 / D),
              after=[mm_s], inc=True)
    a2 = P.op("scalar", lambda E: E.mul(ms_ap, B["ps_sq"][0:1, :w], 1.0 / D),
              after=[mm_q], inc=True)
    P.op("vector", lambda E: E.tensor_mul(var_ap, m_ap, m_ap), after=[a1])
    v2 = P.op("vector", lambda E: E.tensor_sub(var_ap, ms_ap, var_ap),
              after=[a2], inc=True)
    a3 = P.op("scalar", lambda E: E.activation(out=rs_ap, in_=var_ap,
                                               func=AF.Sqrt,
                                               bias=C["epsc"][0:1, 0:1],
                                               scale=1.0),
              after=[v2, C["tok"][7]], inc=True)
    a4 = P.op("vector", lambda E: E.reciprocal(rs_ap, rs_ap), after=[a3],
              inc=True)
    bmm = P.op("tensor", lambda E: E.matmul(
        B["ps_sum"][:, :w], C["ones_row"][0:1, :], m_ap, start=True, stop=True),
        after=[a1, C["tok"][4]], inc=True)
    brm = P.op("tensor", lambda E: E.matmul(
        B["ps_sq"][:, :w], C["ones_row"][0:1, :], rs_ap, start=True, stop=True),
        after=[a4], inc=True)
    toks = []
    for oc in range(DC):
        r = r_ap(oc)
        d = dst_ap(oc)
        P.op("vector", lambda E, b=r, d=d: E.tensor_sub(d, b, B["ps_sum"][:, :w]),
             after=[bmm, dst_wars[oc]])
        if affine is None:
            u = P.op("vector", lambda E, d=d: E.tensor_mul(d, d, B["ps_sq"][:, :w]),
                     after=[brm], inc=True)
        else:
            P.op("vector", lambda E, d=d: E.tensor_mul(d, d, B["ps_sq"][:, :w]),
                 after=[brm])
            wap, bap = affine[0](oc), affine[1](oc)
            u = P.op("vector", lambda E, d=d, wa=wap, ba=bap: E.tensor_scalar(
                d, d, wa, ba, ALU.mult, ALU.add), inc=True)
        toks.append(u)
    B["war_sum"] = B["war_sq"] = toks[-1]
    return toks


def emit_stage_a(nc, P, C, dr):
    T = TOWN
    with ExitStack() as stk:
        sb = lambda nm, sh, dt: stk.enter_context(nc.sbuf_tensor(nm, sh, dt)).ap()
        qtb = sb("a_qtb", [128, DC * T], BF16)
        aob = sb("a_aob", [128, DC * T], BF16)
        wkb = sb("a_wkb", [128, CC * 1024], BF16)
        wvb = sb("a_wvb", [128, CC * 1024], BF16)

        tok_wk = [P.dma("gpsimd", wkb[:, cc * 1024:(cc + 1) * 1024],
                        dr["wk"][cc * 128:(cc + 1) * 128, :]) for cc in range(CC)]
        tok_wv = [P.dma("gpsimd", wvb[:, cc * 1024:(cc + 1) * 1024],
                        dr["wv"][cc * 128:(cc + 1) * 128, :]) for cc in range(CC)]

        with ExitStack() as stk1:
            sb1 = lambda nm, sh, dt: stk1.enter_context(nc.sbuf_tensor(nm, sh, dt)).ap()
            ps1 = lambda nm, sh, dt: stk1.enter_context(nc.psum_tensor(nm, sh, dt)).ap()
            xbf = sb1("a_xbf", [128, DC * T], BF16)
            wqb = sb1("a_wqb", [128, DC * 1024], BF16)
            pq = [ps1("a_pq0", [128, 512], F32), ps1("a_pq1", [128, 512], F32)]
            tok_x = [P.dma("gpsimd", xbf[:, kc * T:(kc + 1) * T],
                           dr["xT"][kc * 128:(kc + 1) * 128, :]) for kc in range(DC)]
            tok_wq = [P.dma("gpsimd", wqb[:, kc * 1024:(kc + 1) * 1024],
                            dr["wq"][kc * 128:(kc + 1) * 128, :]) for kc in range(DC)]
            rng = Ring(2)
            for oc in range(DC):
                for (t0, w) in tiles_of(T, 512):
                    i, war = rng.next()
                    bank = pq[i][:, :w]
                    mm = None
                    for kc in range(DC):
                        mm = P.op("tensor", lambda E, b=bank, kc=kc, oc=oc, t0=t0, w=w: E.matmul(
                            b, wqb[:, kc * 1024 + oc * 128:kc * 1024 + (oc + 1) * 128],
                            xbf[:, kc * T + t0:kc * T + t0 + w],
                            start=(kc == 0), stop=(kc == DC - 1)),
                            after=[tok_wq[kc], tok_x[kc]] + ([war] if kc == 0 else []),
                            inc=(kc == DC - 1))
                    ev = P.op("scalar", lambda E, b=bank, oc=oc, t0=t0, w=w: E.copy(
                        qtb[:, oc * T + t0:oc * T + t0 + w], b), after=[mm], inc=True)
                    rng.set(i, ev)
            a1_done = ev
        nrm_toks = {}
        with ExitStack() as stk2:
            sb2 = lambda nm, sh, dt: stk2.enter_context(nc.sbuf_tensor(nm, sh, dt)).ap()
            ps2 = lambda nm, sh, dt: stk2.enter_context(nc.psum_tensor(nm, sh, dt)).ap()
            ktb = sb2("a_ktb", [128, M], BF16)
            vtb = sb2("a_vtb", [128, M], BF16)
            kt2 = [sb2("a_kt20", [128, M], BF16), sb2("a_kt21", [128, M], BF16)]
            vt2 = [sb2("a_vt20", [128, M], BF16), sb2("a_vt21", [128, M], BF16)]
            vsc = sb2("a_vsc", [128, 2 * 512], BF16)
            invB = sb2("a_invB", [128, 512], F32)
            inv1 = sb2("a_inv1", [1, 512], F32)
            un = sb2("a_un", [128, CC * M], BF16)
            pqk = [ps2("a_pqk0", [128, 512], F32), ps2("a_pqk1", [128, 512], F32)]
            ps_sum = ps2("a_pssum", [1, 512], F32)
            pb = ps2("a_pb", [128, 512], F32)
            pav = ps2("a_pav", [128, 512], F32)
            ptp0 = ps2("a_ptp0", [128, 128], BF16)
            ptp1 = ps2("a_ptp1", [128, 128], BF16)
            ptps = [ptp0, ptp1]
            kTd = nc.dram_tensor("a_kTd", [HEADS, 128, M], BF16)
            vTd = nc.dram_tensor("a_vTd", [HEADS, 128, M], BF16)
            tok_ctx = []
            for cc in range(CC):
                tok_ctx.append(P.dma(
                    "gpsimd", un[:, cc * M:(cc + 1) * M],
                    dr["ctxT"][cc * 128:(cc + 1) * 128, :],
                    after=[a1_done]))
            kv_st = {}
            kv_rng = Ring(2)
            last_kv_mm = None
            ktb_war = None
            vtb_war = None
            tp_toks = []
            for h in range(HEADS):
                ev_k = [None] * 8
                vsc_war = [None, None]
                for jt in range(8):
                    i, war = kv_rng.next()
                    bank = pqk[i][:, :]
                    mm = None
                    for cc in range(CC):
                        mm = P.op("tensor", lambda E, b=bank, cc=cc, h=h, jt=jt: E.matmul(
                            b, wkb[:, cc * 1024 + h * 128:cc * 1024 + (h + 1) * 128],
                            un[:, cc * M + jt * 512:cc * M + (jt + 1) * 512],
                            start=(cc == 0), stop=(cc == CC - 1)),
                            after=[tok_ctx[cc], tok_wk[cc], war] if cc == 0 else
                                  [tok_ctx[cc], tok_wk[cc]],
                            inc=(cc == CC - 1))
                    ev_k[jt] = P.op("scalar", lambda E, b=bank, jt=jt: E.copy(
                        ktb[:, jt * 512:(jt + 1) * 512], b),
                        after=[mm, ktb_war], inc=True)
                    kv_rng.set(i, ev_k[jt])
                    i2, war2 = kv_rng.next()
                    bank2 = pqk[i2][:, :]
                    mm2 = None
                    for cc in range(CC):
                        mm2 = P.op("tensor", lambda E, b=bank2, cc=cc, h=h, jt=jt: E.matmul(
                            b, wvb[:, cc * 1024 + h * 128:cc * 1024 + (h + 1) * 128],
                            un[:, cc * M + jt * 512:cc * M + (jt + 1) * 512],
                            start=(cc == 0), stop=(cc == CC - 1)),
                            after=[tok_wv[cc], war2] if cc == 0 else [tok_wv[cc]],
                            inc=(cc == CC - 1))
                    last_kv_mm = mm2
                    vslot = jt % 2
                    evv = P.op("scalar", lambda E, b=bank2, s=vslot: E.copy(
                        vsc[:, s * 512:(s + 1) * 512], b),
                        after=[mm2, vsc_war[vslot]], inc=True)
                    kv_rng.set(i2, evv)
                    last_tp_ev = None
                    for q in range(4):
                        kt = jt * 4 + q
                        pslot = kt % 2
                        tp = P.op("tensor", lambda E, s=vslot, q=q, p=pslot: E.transpose(
                            ptps[p][:, :],
                            vsc[:, s * 512 + q * 128:s * 512 + (q + 1) * 128],
                            C["identb"][:, :]),
                            after=[evv, C["tok"][0],
                                   tp_toks[-2] if len(tp_toks) >= 2 else None],
                            inc=True)
                        evt = P.op("vector", lambda E, kt=kt, p=pslot: E.tensor_copy(
                            vtb[:, kt * 128:(kt + 1) * 128],
                            ptps[p][:, :]),
                            after=[tp, vtb_war], inc=True)
                        tp_toks.append(evt)
                        last_tp_ev = evt
                    vsc_war[vslot] = last_tp_ev
                dk = P.dma("sync", kTd.ap()[h, :, :], ktb[:, :], after=[ev_k[7]])
                dv = P.dma("sync", vTd.ap()[h, :, :], vtb[:, :], after=[tp_toks[-1]])
                kv_st[h] = (dk, dv)
                ktb_war = dk
                vtb_war = dv
            buf_war = [None, None]
            kv_ld = {}
            for h in range(HEADS):
                bi = h % 2
                kv_ld[h] = (
                    P.dma("gpsimd", kt2[bi][:, :], kTd.ap()[h, :, :],
                          after=[kv_st[h][0], buf_war[bi]]),
                    P.dma("gpsimd", vt2[bi][:, :], vTd.ap()[h, :, :],
                          after=[kv_st[h][1], buf_war[bi]]),
                )
                exp_rng = [None, None]
                prev_recip = None
                prev_evb = None
                prev_nrm = None
                first_exp_extra = [last_kv_mm]
                last_head_mm = None
                for ti, (t0, w) in enumerate(tiles_of(T, 512)):
                    exp_toks = []

                    def emit_qk(kt, h=h, bi=bi, t0=t0, w=w):
                        pslot = kt % 2
                        bank = pqk[pslot][:, :w]
                        mm = P.op("tensor", lambda E, b=bank, kt=kt, bi=bi, h=h, t0=t0, w=w: E.matmul(
                            b, kt2[bi][:, kt * 128:(kt + 1) * 128],
                            qtb[:, h * T + t0:h * T + t0 + w],
                            start=True, stop=True),
                            after=[kv_ld[h][0], exp_rng[pslot]], inc=True)
                        ex = P.op("scalar", lambda E, b=bank, kt=kt, w=w: E.activation(
                            out=un[:, kt * 512:kt * 512 + w], in_=b, func=AF.Exp,
                            scale=SCALE_A),
                            after=[mm] + (first_exp_extra[0] if kt == 0 else []),
                            inc=True)
                        first_exp_extra[0] = []
                        exp_rng[pslot] = ex
                        exp_toks.append(ex)

                    def emit_sums(kt, w=w):
                        return P.op("tensor", lambda E, kt=kt, w=w: E.matmul(
                            ps_sum[0:1, :w], C["ones_col_b"][:, 0:1],
                            un[:, kt * 512:kt * 512 + w],
                            start=(kt == 0), stop=(kt == 31)),
                            after=[exp_toks[kt], C["tok"][2]] +
                                  ([prev_recip] if kt == 0 else []),
                            inc=(kt == 31))

                    def emit_av(kt, bi=bi, w=w):
                        return P.op("tensor", lambda E, kt=kt, bi=bi, w=w: E.matmul(
                            pav[:, :w], vt2[bi][:, kt * 128:(kt + 1) * 128],
                            un[:, kt * 512:kt * 512 + w],
                            start=(kt == 0), stop=(kt == 31)),
                            after=[exp_toks[kt], kv_ld[h][1]] +
                                  ([prev_nrm] if kt == 0 else []),
                            inc=(kt == 31))

                    first_exp_extra = [first_exp_extra]
                    LAG = 2
                    mm_s = mm_av = None
                    for kt in range(32 + LAG):
                        if kt < 32:
                            emit_qk(kt)
                        if kt >= LAG:
                            mm_s = emit_sums(kt - LAG)
                            mm_av = emit_av(kt - LAG)
                    first_exp_extra = first_exp_extra[0]
                    rc = P.op("vector", lambda E, w=w: E.reciprocal(
                        inv1[0:1, :w], ps_sum[0:1, :w]), after=[mm_s], inc=True)
                    prev_recip = rc
                    mb = P.op("tensor", lambda E, w=w: E.matmul(
                        pb[:, :w], C["ones_row"][0:1, :], inv1[0:1, :w],
                        start=True, stop=True),
                        after=[rc, prev_evb, C["tok"][4]], inc=True)
                    evb = P.op("scalar", lambda E, w=w: E.copy(invB[:, :w], pb[:, :w]),
                               after=[mb, prev_nrm], inc=True)
                    prev_evb = evb
                    last_head_mm = mm_av
                    nrm = P.op("vector", lambda E, h=h, t0=t0, w=w: E.tensor_tensor(
                        aob[:, h * T + t0:h * T + t0 + w], pav[:, :w], invB[:, :w],
                        ALU.mult), after=[mm_av, evb], inc=True)
                    prev_nrm = nrm
                    nrm_toks[(h, ti)] = nrm
                buf_war[bi] = last_head_mm
        P.barrier()
        with ExitStack() as stk3:
            sb3 = lambda nm, sh, dt: stk3.enter_context(nc.sbuf_tensor(nm, sh, dt)).ap()
            ps3 = lambda nm, sh, dt: stk3.enter_context(nc.psum_tensor(nm, sh, dt)).ap()
            wob = sb3("a_wob", [128, DC * 1024], BF16)
            bout = sb3("a_bout", [128, 8], F32)
            rbuf2 = [sb3("a_rbufa", [128, DC * 512], F32), sb3("a_rbufb", [128, DC * 512], F32)]
            outs2 = [sb3("a_outsa", [128, DC * 512], F32), sb3("a_outsb", [128, DC * 512], F32)]
            rx = sb3("a_rx", [128, 4 * 512], F32)
            pproj = [ps3("a_pproj0", [128, 512], F32), ps3("a_pproj1", [128, 512], F32)]
            lnb = make_ln_bufs(nc, stk3, "a_ln")
            if dr.get("dbg"):
                lnb["dbg_st"] = dr["dbg_st"][:, :]
            tok_wo = [P.dma("gpsimd", wob[:, kc * 1024:(kc + 1) * 1024],
                            dr["wo"][kc * 128:(kc + 1) * 128, :]) for kc in range(DC)]
            tok_bo = P.dma("sync", bout[:, :], dr["boutT"][:, :])
            prng = Ring(2)
            xrng = Ring(4)
            prev_store = [[None] * DC, [None] * DC]
            prev_apply = [[None] * DC, [None] * DC]
            for ti, (t0, w) in enumerate(tiles_of(T, 512)):
                par = ti % 2
                rbuf = rbuf2[par]
                outs = outs2[par]
                radd = [None] * DC
                for oc in range(DC):
                    i, war = prng.next()
                    bank = pproj[i][:, :w]
                    mm = None
                    for kc in range(DC):
                        mm = P.op("tensor", lambda E, b=bank, kc=kc, oc=oc, t0=t0, w=w: E.matmul(
                            b, wob[:, kc * 1024 + oc * 128:kc * 1024 + (oc + 1) * 128],
                            aob[:, kc * T + t0:kc * T + t0 + w],
                            start=(kc == 0), stop=(kc == DC - 1)),
                            after=[tok_wo[kc], nrm_toks[(kc, ti)]] +
                                  ([war] if kc == 0 else []),
                            inc=(kc == DC - 1))
                    ev = P.op("scalar", lambda E, b=bank, r=rbuf, oc=oc, w=w: E.activation(
                        out=r[:, oc * 512:oc * 512 + w], in_=b, func=AF.Identity,
                        bias=bout[:, oc:oc + 1], scale=1.0),
                        after=[mm, tok_bo, prev_apply[par][oc]], inc=True)
                    prng.set(i, ev)
                    xi, xwar = xrng.next()
                    xd = P.dma("sync", rx[:, xi * 512:xi * 512 + w],
                               dr["xT"][oc * 128:(oc + 1) * 128, t0:t0 + w],
                               after=[xwar])
                    ra = P.op("vector", lambda E, r=rbuf, oc=oc, xi=xi, w=w: E.tensor_add(
                        r[:, oc * 512:oc * 512 + w],
                        r[:, oc * 512:oc * 512 + w],
                        rx[:, xi * 512:xi * 512 + w]),
                        after=[ev, xd], inc=True)
                    xrng.set(xi, ra)
                    radd[oc] = ra
                r_ap = lambda oc, r=rbuf: r[:, oc * 512:oc * 512 + w]
                dst_ap = lambda oc, o=outs: o[:, oc * 512:oc * 512 + w]
                apply_toks = emit_ln_gated(P, C, lnb, w, r_ap, dst_ap, radd,
                                           prev_store[par])
                for oc in range(DC):
                    st = P.dma("sync",
                               dr["x1T"][oc * 128:(oc + 1) * 128, OFF + t0:OFF + t0 + w],
                               outs[:, oc * 512:oc * 512 + w],
                               after=[apply_toks[oc]])
                    prev_store[par][oc] = st
                    prev_apply[par][oc] = apply_toks[oc]


CAP = 768
BIGOFF = 65536.0

def emit_stage_b(nc, P, C, dr, tail_hook=None):
    T = TOWN
    NTT = T // 128
    with ExitStack() as stk:
        sb = lambda nm, sh, dt: stk.enter_context(nc.sbuf_tensor(nm, sh, dt)).ap()
        if tail_hook is not None:
            P.halo_sbuf = (sb("x_si", [128, 1], I32),
                           sb("x_hs", [128, DC * OFF], F32))
        wr_sb = sb("b_wr", [128, DC * 8], BF16)
        b1s = sb("b_b1s", [128, NE * 32], F32)
        b2s = sb("b_b2s", [128, NE * 8], F32)
        x1b = sb("b_x1b", [128, DC * T], BF16)
        psm = sb("b_psm", [128, NTT], F32)
        idx = sb("b_idx", [128, NTT], I32)
        zt = sb("b_zt", [128, 1024], BF16)
        tok_wr = [P.dma("gpsimd", wr_sb[:, kc * 8:(kc + 1) * 8],
                        dr["wr"][kc * 128:(kc + 1) * 128, :]) for kc in range(DC)]
        tok_b1 = [P.dma("sync", b1s[:, e * 32:(e + 1) * 32], dr["b1moeT"][e, :, :])
                  for e in range(NE)]
        tok_b2 = [P.dma("sync", b2s[:, e * 8:(e + 1) * 8], dr["b2moeT"][e, :, :])
                  for e in range(NE)]
        tok_x1b = [P.dma("gpsimd", x1b[:, kc * T:(kc + 1) * T],
                         dr["x1T"][kc * 128:(kc + 1) * 128, OFF:OFF + T])
                   for kc in range(DC)]
        zm = P.op("vector", lambda E: E.memset(zt[:, :], 0.0), inc=True)
        ztoks = []
        for i in range(NE * CAP // 128):
            ztoks.append(P.dma("sync", dr["xg"][i * 128:(i + 1) * 128, :], zt[:, :],
                               after=[zm]))
            ztoks.append(P.dma("sync", dr["og"][i * 128:(i + 1) * 128, :], zt[:, :],
                               after=[zm]))
        with ExitStack() as stkr:
            sbr = lambda nm, sh, dt: stkr.enter_context(nc.sbuf_tensor(nm, sh, dt)).ap()
            psr = lambda nm, sh, dt: stkr.enter_context(nc.psum_tensor(nm, sh, dt)).ap()
            lg = sbr("b_lg", [8, T], F32)
            pr = psr("b_pr", [8, 512], F32)
            ev_lg = None
            prr = Ring(1)
            for (t0, w) in tiles_of(T, 512):
                mm = None
                i, war = prr.next()
                for kc in range(DC):
                    mm = P.op("tensor", lambda E, kc=kc, t0=t0, w=w: E.matmul(
                        pr[0:8, :w], wr_sb[:, kc * 8:(kc + 1) * 8],
                        x1b[:, kc * T + t0:kc * T + t0 + w],
                        start=(kc == 0), stop=(kc == DC - 1)),
                        after=[tok_wr[kc], tok_x1b[kc]] + ([war] if kc == 0 else []),
                        inc=(kc == DC - 1))
                ev_lg = P.op("scalar", lambda E, t0=t0, w=w: E.copy(
                    lg[0:8, t0:t0 + w], pr[0:8, :w]), after=[mm], inc=True)
                prr.set(i, ev_lg)
            if dr.get("dbg"):
                P.dma("sync", dr["dbg_lg"][:, :], lg[0:8, :], after=[ev_lg])
            lt_all = sbr("b_ltall", [128, NTT * 8], F32)
            dd_all = sbr("b_ddall", [128, NTT * 8], F32)
            oh_all = sbr("b_ohall", [128, NTT * 8], F32)
            eo_all = sbr("b_eoall", [128, NTT * 8], F32)
            nmx = sbr("b_nmx", [128, NTT], F32)
            cum_all = sbr("b_cumall", [128, NTT * 8], F32)
            tot_sb = sbr("b_totsb", [1, NTT * 8], F32)
            base_sb = sbr("b_basesb", [1, NTT * 8], F32)
            bb_all = sbr("b_bball", [128, NTT * 8], F32)
            rank_all = sbr("b_rank", [128, NTT], F32)
            eid_all = sbr("b_eid", [128, NTT], F32)
            slot_all = sbr("b_slot", [128, NTT], F32)
            evt_toks = []
            trng_r = Ring(2)
            ptr2 = [psr("b_ptr0", [128, 8], F32), psr("b_ptr1", [128, 8], F32)]
            for tt in range(NTT):
                ri, rwar = trng_r.next()
                tp = P.op("tensor", lambda E, tt=tt, ri=ri: E.transpose(
                    ptr2[ri][:, :], lg[0:8, tt * 128:(tt + 1) * 128],
                    C["identf"][0:8, 0:8]),
                    after=[ev_lg, C["tok"][1], rwar], inc=True)
                evt = P.op("vector", lambda E, tt=tt, ri=ri: E.tensor_copy(
                    lt_all[:, tt * 8:(tt + 1) * 8], ptr2[ri][:, :]),
                    after=[tp], inc=True)
                trng_r.set(ri, evt)
                evt_toks.append(evt)
            last_dd = None
            mx_toks = []
            for tt in range(NTT):
                mx_toks.append(P.op("vector", lambda E, tt=tt: E.reduce_max(
                    nmx[:, tt:tt + 1], lt_all[:, tt * 8:(tt + 1) * 8],
                    axis=AXL, negate=True), after=[evt_toks[tt]], inc=True))
            for tt in range(NTT):
                last_dd = P.op("vector", lambda E, tt=tt: E.tensor_scalar_add(
                    dd_all[:, tt * 8:(tt + 1) * 8],
                    lt_all[:, tt * 8:(tt + 1) * 8], nmx[:, tt:tt + 1]),
                    force_after=[mx_toks[-1]] if tt == 0 else (),
                    inc=(tt == NTT - 1))
            oh_tok = P.op("vector", lambda E: E.tensor_single_scalar(
                oh_all[:, :], dd_all[:, :], 0.0, ALU.is_ge), inc=True)
            eo_tok = P.op("scalar", lambda E: E.activation(
                out=eo_all[:, :], in_=dd_all[:, :], func=AF.Exp),
                after=[last_dd], inc=True)
            sm_tok = P.op("vector", lambda E: E.reduce_sum(
                psm.rearrange("p (t o) -> p t o", o=1),
                eo_all.rearrange("p (t o) -> p t o", o=8), axis=AXL),
                after=[eo_tok], inc=True)
            rc_tok = P.op("vector", lambda E: E.reciprocal(psm[:, :], psm[:, :]),
                          force_after=[sm_tok], inc=True)
            psm_done = rc_tok
            pc2 = [psr("b_pc0", [128, 8], F32), psr("b_pc1", [128, 8], F32)]
            ptot = psr("b_ptot", [1, NTT * 8], F32)
            crng = Ring(2)
            tot_mms = []
            for tt in range(NTT):
                ci, cwar = crng.next()
                mc = P.op("tensor", lambda E, tt=tt, ci=ci: E.matmul(
                    pc2[ci][:, :], C["triuS"][:, :], oh_all[:, tt * 8:(tt + 1) * 8],
                    start=True, stop=True),
                    after=[oh_tok, C["tok"][5], cwar], inc=True)
                ec = P.op("scalar", lambda E, tt=tt, ci=ci: E.copy(
                    cum_all[:, tt * 8:(tt + 1) * 8], pc2[ci][:, :]),
                    after=[mc], inc=True)
                crng.set(ci, ec)
                mt = P.op("tensor", lambda E, tt=tt: E.matmul(
                    ptot[0:1, tt * 8:(tt + 1) * 8], C["ones_col_f"][:, 0:1],
                    oh_all[:, tt * 8:(tt + 1) * 8], start=True, stop=True),
                    after=[oh_tok, C["tok"][3]], inc=(tt == NTT - 1))
                tot_mms.append(mt)
            et = P.op("scalar", lambda E: E.copy(tot_sb[0:1, :], ptot[0:1, :]),
                      after=[tot_mms[-1]], inc=True)
            bz = P.op("vector", lambda E: E.memset(base_sb[0:1, 0:8], 0.0), inc=True)
            prev_b = bz
            for tt in range(1, NTT):
                prev_b = P.op("vector", lambda E, tt=tt: E.tensor_add(
                    base_sb[0:1, tt * 8:(tt + 1) * 8],
                    base_sb[0:1, (tt - 1) * 8:tt * 8],
                    tot_sb[0:1, (tt - 1) * 8:tt * 8]),
                    after=[et], force_after=[prev_b], inc=True)
            pbb2 = [psr("b_pbb0", [128, 8], F32), psr("b_pbb1", [128, 8], F32)]
            brng = Ring(2)
            bb_toks = []
            for tt in range(NTT):
                bi, bwar = brng.next()
                mb = P.op("tensor", lambda E, tt=tt, bi=bi: E.matmul(
                    pbb2[bi][:, :], C["ones_row"][0:1, :],
                    base_sb[0:1, tt * 8:(tt + 1) * 8], start=True, stop=True),
                    after=[prev_b, bwar, C["tok"][4]], inc=True)
                eb = P.op("scalar", lambda E, tt=tt, bi=bi: E.copy(
                    bb_all[:, tt * 8:(tt + 1) * 8], pbb2[bi][:, :]),
                    after=[mb], inc=True)
                brng.set(bi, eb)
                bb_toks.append(eb)
            m1 = P.op("vector", lambda E: E.tensor_add(bb_all[:, :], bb_all[:, :],
                                                       cum_all[:, :]),
                      after=[bb_toks[-1], ec], inc=True)
            m2 = P.op("vector", lambda E: E.tensor_mul(bb_all[:, :], bb_all[:, :],
                                                       oh_all[:, :]), inc=True)
            m3 = P.op("vector", lambda E: E.reduce_sum(
                rank_all.rearrange("p (t o) -> p t o", o=1),
                bb_all.rearrange("p (t o) -> p t o", o=8), axis=AXL),
                inc=True)
            m4 = P.op("vector", lambda E: E.tensor_tensor(
                dd_all[:, :], oh_all[:, :], C["erowT"][:, 0:NTT * 8], ALU.mult),
                after=[C["tok"][6], eo_tok], inc=True)
            m5 = P.op("vector", lambda E: E.reduce_sum(
                eid_all.rearrange("p (t o) -> p t o", o=1),
                dd_all.rearrange("p (t o) -> p t o", o=8), axis=AXL),
                inc=True)
            m6 = P.op("vector", lambda E: E.tensor_single_scalar(
                slot_all[:, :], rank_all[:, :], float(CAP), ALU.is_ge),
                force_after=[m3], inc=True)
            m7 = P.op("vector", lambda E: E.scalar_tensor_tensor(
                slot_all[:, :], slot_all[:, :], BIGOFF, rank_all[:, :],
                ALU.mult, ALU.add), force_after=[m6], inc=True)
            m8 = P.op("vector", lambda E: E.scalar_tensor_tensor(
                slot_all[:, :], eid_all[:, :], float(CAP), slot_all[:, :],
                ALU.mult, ALU.add), force_after=[m5, m7], inc=True)
            cvt = P.op("vector", lambda E: E.tensor_copy(idx[:, :], slot_all[:, :]),
                       force_after=[m8], inc=True)
            if dr.get("dbg"):
                P.dma("sync", dr["dbg_cum"][:, :], cum_all[:, :], after=[ec])
                P.dma("sync", dr["dbg_tot"][:, :], tot_sb[0:1, :], after=[et])
                P.dma("sync", dr["dbg_base"][:, :], base_sb[0:1, :], after=[prev_b])
                P.dma("sync", dr["dbg_idx"][:, :], idx[:, :], after=[cvt])
                P.dma("sync", dr["dbg_rank"][:, :], rank_all[:, :], after=[m8])
                P.dma("sync", dr["dbg_eid"][:, :], eid_all[:, :], after=[m8])
                P.dma("sync", dr["dbg_oh"][:, :], oh_all[:, :], after=[oh_tok])
                P.dma("sync", dr["dbg_pm"][:, :], psm[:, :], after=[rc_tok])
        with ExitStack() as stks:
            sbs = lambda nm, sh, dt: stks.enter_context(nc.sbuf_tensor(nm, sh, dt)).ap()
            pss = lambda nm, sh, dt: stks.enter_context(nc.psum_tensor(nm, sh, dt)).ap()
            ttile = [sbs("b_tt0", [128, 1024], BF16), sbs("b_tt1", [128, 1024], BF16)]
            ptv = [pss("b_ptv0", [128, 128], BF16), pss("b_ptv1", [128, 128], BF16)]
            trng = Ring(2)
            srng = Ring(2)
            scat_toks = []
            for tt in range(NTT):
                si_, swar = srng.next()
                evs = []
                for kc in range(DC):
                    ti_, twar = trng.next()
                    tp = P.op("tensor", lambda E, tt=tt, kc=kc, ti_=ti_: E.transpose(
                        ptv[ti_][:, :], x1b[:, kc * T + tt * 128:kc * T + (tt + 1) * 128],
                        C["identb"][:, :]),
                        after=[tok_x1b[kc], C["tok"][0], twar] +
                              ([swar] if kc == 0 else []), inc=True)
                    ev = P.op("vector", lambda E, kc=kc, si_=si_, ti_=ti_: E.tensor_copy(
                        ttile[si_][:, kc * 128:(kc + 1) * 128], ptv[ti_][:, :]),
                        after=[tp], inc=True)
                    trng.set(ti_, ev)
                    evs.append(ev)
                sc = P.raw_dma("gpsimd", (lambda E, tt=tt, si_=si_:
                    E.indirect_dma_start(
                        dr["xg"][:, :], bass.IndirectOffsetOnAxis(idx[:, tt:tt + 1], 0),
                        ttile[si_][:, :], None,
                        bounds_check=NE * CAP, oob_is_err=False)),
                    after=evs + [cvt] + ztoks[-8:])
                srng.set(si_, sc)
                scat_toks.append(sc)
        with ExitStack() as stk2:
            sb2 = lambda nm, sh, dt: stk2.enter_context(nc.sbuf_tensor(nm, sh, dt)).ap()
            ps2 = lambda nm, sh, dt: stk2.enter_context(nc.psum_tensor(nm, sh, dt)).ap()
            xgt = [sb2("b_xgt0", [128, 1024], BF16), sb2("b_xgt1", [128, 1024], BF16)]
            otm = [sb2("b_otm0", [128, 1024], BF16), sb2("b_otm1", [128, 1024], BF16)]
            xbf = sb2("b_xbf", [128, DC * CAP], BF16)
            hT = sb2("b_hT", [128, FC * CAP], BF16)
            oT = sb2("b_oT", [128, DC * CAP], BF16)
            w1r = [sb2("b_w1r0", [128, DC * 128], BF16), sb2("b_w1r1", [128, DC * 128], BF16)]
            w2r = [sb2("b_w2r0", [128, FC * 128], BF16), sb2("b_w2r1", [128, FC * 128], BF16)]
            ptv2 = [ps2("b_ptw0", [128, 128], BF16), ps2("b_ptw1", [128, 128], BF16)]
            pe1 = [ps2("b_pe10", [128, 512], F32), ps2("b_pe11", [128, 512], F32)]
            pe2 = [ps2("b_pe20", [128, 512], F32), ps2("b_pe21", [128, 512], F32)]
            grng = Ring(2)
            orng = Ring(2)
            t2rng = Ring(2)
            w1rng = Ring(2)
            w2rng = Ring(2)
            p1rng = Ring(2)
            p2rng = Ring(2)
            og_toks = {}
            for e in range(NE):
                xin_toks = []
                for i in range(CAP // 128):
                    gi, gwar = grng.next()
                    gl = P.dma("gpsimd", xgt[gi][:, :],
                               dr["xg"][e * CAP + i * 128:e * CAP + (i + 1) * 128, :],
                               after=[gwar] + scat_toks)
                    last_ev = None
                    for kc in range(DC):
                        ti_, twar = t2rng.next()
                        tp = P.op("tensor", lambda E, gi=gi, kc=kc, ti_=ti_: E.transpose(
                            ptv2[ti_][:, :], xgt[gi][:, kc * 128:(kc + 1) * 128],
                            C["identb"][:, :]), after=[gl, twar], inc=True)
                        ev = P.op("vector", lambda E, kc=kc, i=i, ti_=ti_: E.tensor_copy(
                            xbf[:, kc * CAP + i * 128:kc * CAP + (i + 1) * 128],
                            ptv2[ti_][:, :]), after=[tp], inc=True)
                        t2rng.set(ti_, ev)
                        last_ev = ev
                        xin_toks.append(ev)
                    grng.set(gi, last_ev)
                ev_h_last = None
                for oc in range(FC):
                    wi, wwar = w1rng.next()
                    dtok = P.dma("gpsimd",
                                 w1r[wi].rearrange("p (kc n) -> p kc n", kc=DC),
                                 dr["w1moe"][e, :, oc * 128:(oc + 1) * 128]
                                 .rearrange("(kc p) n -> p kc n", p=128),
                                 after=[wwar])
                    last_mm = None
                    for (t0, w) in tiles_of(CAP, 512):
                        pi, pwar = p1rng.next()
                        bank = pe1[pi][:, :w]
                        mm = None
                        for kc in range(DC):
                            mm = P.op("tensor", lambda E, b=bank, wi=wi, kc=kc, t0=t0, w=w: E.matmul(
                                b, w1r[wi][:, kc * 128:(kc + 1) * 128],
                                xbf[:, kc * CAP + t0:kc * CAP + t0 + w],
                                start=(kc == 0), stop=(kc == DC - 1)),
                                after=[dtok, xin_toks[-1]] + ([pwar] if kc == 0 else []),
                                inc=(kc == DC - 1))
                        ev = P.op("scalar", lambda E, b=bank, e=e, oc=oc, t0=t0, w=w: E.activation(
                            out=hT[:, oc * CAP + t0:oc * CAP + t0 + w], in_=b,
                            func=AF.Gelu, bias=b1s[:, e * 32 + oc:e * 32 + oc + 1],
                            scale=1.0),
                            after=[mm, tok_b1[e]], inc=True)
                        p1rng.set(pi, ev)
                        last_mm = mm
                        ev_h_last = ev
                    w1rng.set(wi, last_mm)
                oT_toks = []
                for oc in range(DC):
                    wi, wwar = w2rng.next()
                    dtoks = []
                    for kq in range(4):
                        dtoks.append(P.dma(
                            "gpsimd", w2r[wi][:, kq * 8 * 128:(kq + 1) * 8 * 128]
                            .rearrange("p (kc n) -> p kc n", kc=8),
                            dr["w2moe"][e, kq * 1024:(kq + 1) * 1024,
                                        oc * 128:(oc + 1) * 128]
                            .rearrange("(kc p) n -> p kc n", p=128),
                            after=[wwar] if kq == 0 else []))
                    last_mm = None
                    for (t0, w) in tiles_of(CAP, 512):
                        pi, pwar = p2rng.next()
                        bank = pe2[pi][:, :w]
                        mm = None
                        for kc in range(FC):
                            mm = P.op("tensor", lambda E, b=bank, wi=wi, kc=kc, t0=t0, w=w: E.matmul(
                                b, w2r[wi][:, kc * 128:(kc + 1) * 128],
                                hT[:, kc * CAP + t0:kc * CAP + t0 + w],
                                start=(kc == 0), stop=(kc == FC - 1)),
                                after=(dtoks + [ev_h_last, pwar]) if kc == 0 else [],
                                inc=(kc == FC - 1))
                        ev = P.op("scalar", lambda E, b=bank, e=e, oc=oc, t0=t0, w=w: E.activation(
                            out=oT[:, oc * CAP + t0:oc * CAP + t0 + w], in_=b,
                            func=AF.Identity, bias=b2s[:, e * 8 + oc:e * 8 + oc + 1],
                            scale=1.0), after=[mm, tok_b2[e]], inc=True)
                        p2rng.set(pi, ev)
                        last_mm = mm
                        oT_toks.append(ev)
                    w2rng.set(wi, last_mm)
                for i in range(CAP // 128):
                    oi, owar = orng.next()
                    evs = []
                    for kc in range(DC):
                        ti_, twar = t2rng.next()
                        tp = P.op("tensor", lambda E, kc=kc, i=i, ti_=ti_: E.transpose(
                            ptv2[ti_][:, :],
                            oT[:, kc * CAP + i * 128:kc * CAP + (i + 1) * 128],
                            C["identb"][:, :]),
                            after=[oT_toks[-1]] + ([owar] if kc == 0 else []) + [twar],
                            inc=True)
                        ev = P.op("vector", lambda E, kc=kc, oi=oi, ti_=ti_: E.tensor_copy(
                            otm[oi][:, kc * 128:(kc + 1) * 128], ptv2[ti_][:, :]),
                            after=[tp], inc=True)
                        t2rng.set(ti_, ev)
                        evs.append(ev)
                    st_ = P.dma("sync", dr["og"][e * CAP + i * 128:e * CAP + (i + 1) * 128, :],
                                otm[oi][:, :], after=evs + ztoks[-8:])
                    orng.set(oi, st_)
                    og_toks[(e, i)] = st_
        with ExitStack() as stkg:
            sbg = lambda nm, sh, dt: stkg.enter_context(nc.sbuf_tensor(nm, sh, dt)).ap()
            psg = lambda nm, sh, dt: stkg.enter_context(nc.psum_tensor(nm, sh, dt)).ap()
            mtm = [sbg("b_mtm0", [128, 1024], BF16), sbg("b_mtm1", [128, 1024], BF16)]
            rbuf2 = [sbg("b_rbufa", [128, DC * 512], F32), sbg("b_rbufb", [128, DC * 512], F32)]
            outs2 = [sbg("b_outsa", [128, DC * 512], F32), sbg("b_outsb", [128, DC * 512], F32)]
            rx = sbg("b_rx", [128, 4 * 512], F32)
            ptg = [psg("b_ptg0", [128, 128], BF16), psg("b_ptg1", [128, 128], BF16)]
            lnb = make_ln_bufs(nc, stkg, "b_ln")
            mrng = Ring(2)
            tgr = Ring(2)
            xrng = Ring(4)
            prev_store = [[None] * DC, [None] * DC]
            gpar = 0
            groups = tiles_of(T, 512)
            groups = [groups[-1]] + groups[:-1]
            P.x2_tail_stores = {}
            for (g0, gw) in groups:
                par = gpar % 2
                gpar += 1
                rbuf = rbuf2[par]
                outs = outs2[par]
                radd = [None] * DC
                racc = {}
                for qi in range(gw // 128):
                    tt = (g0 + qi * 128) // 128
                    mi, mwar = mrng.next()
                    gm = P.raw_dma("gpsimd", (lambda E, tt=tt, mi=mi:
                        E.indirect_dma_start(
                            mtm[mi][:, :], None, dr["og"][:, :],
                            bass.IndirectOffsetOnAxis(idx[:, tt:tt + 1], 0),
                            bounds_check=NE * CAP, oob_is_err=False)),
                        after=[mwar, psm_done] + list(og_toks.values()))
                    sc = P.op("vector", lambda E, tt=tt, mi=mi: E.tensor_scalar_mul(
                        mtm[mi][:, :], mtm[mi][:, :], psm[:, tt:tt + 1]),
                        after=[gm], inc=True)
                    for kc in range(DC):
                        ti_, twar = tgr.next()
                        tp = P.op("tensor", lambda E, kc=kc, mi=mi, ti_=ti_: E.transpose(
                            ptg[ti_][:, :], mtm[mi][:, kc * 128:(kc + 1) * 128],
                            C["identb"][:, :]), after=[sc, twar], inc=True)
                        xi, xwar = xrng.next()
                        xd = P.dma("sync", rx[:, xi * 512:xi * 512 + 128],
                                   dr["x1T"][kc * 128:(kc + 1) * 128,
                                             OFF + g0 + qi * 128:OFF + g0 + (qi + 1) * 128],
                                   after=[xwar])
                        ra = P.op("vector", lambda E, r=rbuf, kc=kc, qi=qi, xi=xi, ti_=ti_: E.tensor_tensor(
                            r[:, kc * 512 + qi * 128:kc * 512 + (qi + 1) * 128],
                            rx[:, xi * 512:xi * 512 + 128], ptg[ti_][:, :], ALU.add),
                            after=[tp, xd, prev_store[par][kc]], inc=True)
                        tgr.set(ti_, ra)
                        xrng.set(xi, ra)
                        racc[kc] = ra
                    mrng.set(mi, ra)
                r_toks = [racc[oc] for oc in range(DC)]
                r_ap = lambda oc, r=rbuf: r[:, oc * 512:oc * 512 + gw]
                dst_ap = lambda oc, o=outs: o[:, oc * 512:oc * 512 + gw]
                ap_toks = emit_ln_gated(P, C, lnb, gw, r_ap, dst_ap, r_toks,
                                        prev_store[par])
                for oc in range(DC):
                    st_ = P.dma("sync",
                                dr["x2T"][oc * 128:(oc + 1) * 128, OFF + g0:OFF + g0 + gw],
                                outs[:, oc * 512:oc * 512 + gw], after=[ap_toks[oc]])
                    prev_store[par][oc] = st_
                    if g0 + gw == T:
                        P.x2_tail_stores[oc] = st_
                if tail_hook is not None and g0 + gw == T:
                    tail_hook()


def emit_halo_issue(nc, P, C, dr, bufs):
    si, hs = bufs
    own = dr["x2tail"]
    gat = dr["x2gat"]
    tail = getattr(P, "x2_tail_stores", {})
    sid = P.dma("sync", si[:, :], dr["halo_idx"][:, :])
    ttoks = []
    for kc in range(DC):
        ttoks.append(P.dma("sync", own.ap()[0:128, kc * OFF:(kc + 1) * OFF],
                           dr["x2T"][kc * 128:(kc + 1) * 128, T - OFF:T],
                           after=[tail.get(kc)]))
    cc = P.op("gpsimd", lambda E: E.collective_compute(
        "AllGather", ALU.bypass, [[0, 1, 2, 3, 4, 5, 6, 7]],
        [own.ap().opt()], [gat.ap().opt()]), after=ttoks, inc=True)
    mz = P.op("vector", lambda E: E.memset(hs[:, :], 0.0), inc=True)
    P.halo_state = (cc, sid, mz)


def emit_halo_exchange(nc, P, C, dr, bufs):
    si, hs = bufs
    gat = dr["x2gat"]
    cc, sid, mz = P.halo_state
    gm = P.raw_dma("gpsimd", (lambda E: E.indirect_dma_start(
        hs[:, :], None, gat.ap()[:, :],
        bass.IndirectOffsetOnAxis(si[:, 0:1], 0),
        bounds_check=8 * 128, oob_is_err=False)),
        after=[sid, mz], force_after=[cc])
    for kc in range(DC):
        P.dma("scalar", dr["x2T"][kc * 128:(kc + 1) * 128, 0:OFF],
              hs[:, kc * OFF:(kc + 1) * OFF], after=[gm])


def emit_stage_c(nc, P, C, dr):
    NQT = T // 128
    with ExitStack() as stk:
        sb = lambda nm, sh, dt: stk.enter_context(nc.sbuf_tensor(nm, sh, dt)).ap()
        ps = lambda nm, sh, dt: stk.enter_context(nc.psum_tensor(nm, sh, dt)).ap()
        qb = sb("c_qb", [128, DC * T], BF16)
        kb = sb("c_kb", [128, DC * T], BF16)
        vtmb = sb("c_vtmb", [128, NQT * DC * 128], BF16)
        bqkv = sb("c_bqkv", [128, 24], F32)
        lnw = sb("c_lnw", [128, 8], F32)
        lnbw = sb("c_lnbw", [128, 8], F32)
        msk = sb("c_msk", [128, 2 * 128], F32)
        hmask = sb("c_hmask", [128, 1], F32)
        tok_small = [
            P.dma("sync", bqkv[:, :], dr["bqkvT"][:, :]),
            P.dma("sync", lnw[:, :], dr["lnwT"][:, :]),
            P.dma("sync", lnbw[:, :], dr["lnbT"][:, :]),
            P.dma("sync", msk[:, 0:128], dr["mskL"][:, :]),
            P.dma("sync", msk[:, 128:256], dr["mskU"][:, :]),
            P.dma("sync", hmask[:, :], dr["hmask"][:, :]),
        ]
        ev_q = {}
        ev_k = {}
        ev_v = {}
        with ExitStack() as stk1:
            sb1 = lambda nm, sh, dt: stk1.enter_context(nc.sbuf_tensor(nm, sh, dt)).ap()
            ps1 = lambda nm, sh, dt: stk1.enter_context(nc.psum_tensor(nm, sh, dt)).ap()
            x2b = sb1("c_x2b", [128, DC * T], BF16)
            wr_ = [sb1("c_wr0", [128, DC * 128], BF16),
                   sb1("c_wr1", [128, DC * 128], BF16)]
            vsc = sb1("c_vsc", [128, 2 * 512], BF16)
            pp = [ps1("c_pp0", [128, 512], F32), ps1("c_pp1", [128, 512], F32)]
            ptv = [ps1("c_ptv0", [128, 128], BF16), ps1("c_ptv1", [128, 128], BF16)]
            tok_x2 = {}
            for (t0, w) in tiles_of(T, 512):
                for kc in range(DC):
                    tok_x2[(kc, t0)] = P.dma(
                        "gpsimd", x2b[:, kc * T + t0:kc * T + t0 + w],
                        dr["x2T"][kc * 128:(kc + 1) * 128, t0:t0 + w])
            wrng = Ring(2)
            prng = Ring(2)
            vrng = Ring(2)
            tvrng = Ring(2)
            for oc in range(24):
                wi, wwar = wrng.next()
                dtok = P.dma("gpsimd",
                             wr_[wi].rearrange("p (kc n) -> p kc n", kc=DC),
                             dr["wqkv"][:, oc * 128:(oc + 1) * 128]
                             .rearrange("(kc p) n -> p kc n", p=128),
                             after=[wwar])
                last_mm = None
                for (t0, w) in tiles_of(T, 512):
                    pi, pwar = prng.next()
                    bank = pp[pi][:, :w]
                    mm = None
                    for kc in range(DC):
                        mm = P.op("tensor", lambda E, b=bank, wi=wi, kc=kc, t0=t0, w=w: E.matmul(
                            b, wr_[wi][:, kc * 128:(kc + 1) * 128],
                            x2b[:, kc * T + t0:kc * T + t0 + w],
                            start=(kc == 0), stop=(kc == DC - 1)),
                            after=[dtok, tok_x2[(kc, t0)]] + ([pwar] if kc == 0 else []),
                            inc=(kc == DC - 1))
                    last_mm = mm
                    if oc < 8:
                        ev = P.op("scalar", lambda E, b=bank, oc=oc, t0=t0, w=w: E.activation(
                            out=qb[:, oc * T + t0:oc * T + t0 + w], in_=b,
                            func=AF.Identity, bias=bqkv[:, oc:oc + 1], scale=1.0),
                            after=[mm, tok_small[0]], inc=True)
                        ev_q[(oc, t0)] = ev
                    elif oc < 16:
                        ev = P.op("scalar", lambda E, b=bank, oc=oc, t0=t0, w=w: E.activation(
                            out=kb[:, (oc - 8) * T + t0:(oc - 8) * T + t0 + w], in_=b,
                            func=AF.Identity, bias=bqkv[:, oc:oc + 1], scale=1.0),
                            after=[mm, tok_small[0]], inc=True)
                        if t0 == 0:
                            ev = P.op("vector", lambda E, oc=oc: E.tensor_scalar_mul(
                                kb[:, (oc - 8) * T:(oc - 8) * T + OFF],
                                kb[:, (oc - 8) * T:(oc - 8) * T + OFF],
                                hmask[:, 0:1]), after=[ev, tok_small[5]], inc=True)
                        ev_k[(oc - 8, t0)] = ev
                    else:
                        vi, vwar = vrng.next()
                        vs = vsc[:, vi * 512:vi * 512 + w]
                        ev = P.op("scalar", lambda E, b=bank, s=vs, oc=oc: E.activation(
                            out=s, in_=b, func=AF.Identity,
                            bias=bqkv[:, oc:oc + 1], scale=1.0),
                            after=[mm, tok_small[0], vwar], inc=True)
                        last_evt = None
                        for q in range(w // 128):
                            kt = t0 // 128 + q
                            tvi, tvwar = tvrng.next()
                            tp = P.op("tensor", lambda E, s=vs, q=q, tvi=tvi: E.transpose(
                                ptv[tvi][:, :], s[:, q * 128:(q + 1) * 128],
                                C["identb"][:, :]),
                                after=[ev, C["tok"][0], tvwar], inc=True)
                            evt = P.op("vector", lambda E, kt=kt, oc=oc, tvi=tvi: E.tensor_copy(
                                vtmb[:, (kt * DC + (oc - 16)) * 128:
                                     (kt * DC + (oc - 16)) * 128 + 128],
                                ptv[tvi][:, :]), after=[tp], inc=True)
                            if kt < 2:
                                evt = P.op("vector", lambda E, kt=kt, oc=oc: E.tensor_scalar_mul(
                                    vtmb[:, (kt * DC + (oc - 16)) * 128:
                                         (kt * DC + (oc - 16)) * 128 + 128],
                                    vtmb[:, (kt * DC + (oc - 16)) * 128:
                                         (kt * DC + (oc - 16)) * 128 + 128],
                                    hmask[:, 0:1]), after=[evt, tok_small[5]], inc=True)
                            tvrng.set(tvi, evt)
                            ev_v[(oc - 16, kt)] = evt
                            last_evt = evt
                        vrng.set(vi, last_evt)
                    prng.set(pi, ev)
                wrng.set(wi, last_mm)
            P.barrier()
        with ExitStack() as stk2:
            sb2 = lambda nm, sh, dt: stk2.enter_context(nc.sbuf_tensor(nm, sh, dt)).ap()
            ps2 = lambda nm, sh, dt: stk2.enter_context(nc.psum_tensor(nm, sh, dt)).ap()
            sS = sb2("c_sS", [128, 3 * 128], BF16)
            bnd2 = [sb2("c_bnda", [128, DC * 512], BF16), sb2("c_bndb", [128, DC * 512], BF16)]
            rb2 = sb2("c_rb2", [128, DC * 512], F32)
            outs2 = [sb2("c_outsa", [128, DC * 512], F32), sb2("c_outsb", [128, DC * 512], F32)]
            rx = sb2("c_rx", [128, 2 * 512], F32)
            lnb = make_ln_bufs(nc, stk2, "c_ln")
            pscr = [ps2("c_pscr0", [128, 128], F32), ps2("c_pscr1", [128, 128], F32)]
            pav = [ps2("c_pav0", [128, 128], F32), ps2("c_pav1", [128, 128], F32)]
            srng = Ring(2)
            arng = Ring(2)
            sS_war = [None, None, None]
            prev_store = [[None] * DC, [None] * DC]
            prev_vadd = [None] * DC
            prev_ap1 = [None, None]
            qt_groups = [list(range(g, min(g + 4, NQT))) for g in range(0, NQT, 4)]

            def emit_attn_group(gi_, qg):
                par = gi_ % 2
                bnd = bnd2[par]
                bnd_tok = {}
                nonlocal sS_war
                for qi, qt in enumerate(qg):
                    kts = [(ki, qt - 2 + ki) for ki in range(3) if qt - 2 + ki >= 0]
                    ev_s = []
                    for (ki, kt) in kts:
                        si, swar = srng.next()
                        bank = pscr[si][:, :]
                        mm = None
                        for kc in range(DC):
                            mm = P.op("tensor", lambda E, b=bank, kc=kc, kt=kt, qt=qt: E.matmul(
                                b, kb[:, kc * T + kt * 128:kc * T + (kt + 1) * 128],
                                qb[:, kc * T + qt * 128:kc * T + (qt + 1) * 128],
                                start=(kc == 0), stop=(kc == DC - 1)),
                                after=[ev_k[(kc, (kt * 128 // 512) * 512)],
                                       ev_q[(kc, (qt * 128 // 512) * 512)]] +
                                      ([swar, sS_war[ki]] if kc == 0 else []),
                                inc=(kc == DC - 1))
                        if ki == 1:
                            evs = P.op("scalar", lambda E, b=bank, ki=ki: E.mul(
                                sS[:, ki * 128:(ki + 1) * 128], b, SCALE_C),
                                after=[mm], inc=True)
                        else:
                            mi = 0 if ki == 0 else 1
                            evs = P.op("vector", lambda E, b=bank, ki=ki, mi=mi: E.tensor_tensor(
                                sS[:, ki * 128:(ki + 1) * 128], b,
                                msk[:, mi * 128:(mi + 1) * 128], ALU.mult),
                                after=[mm, tok_small[3]], inc=True)
                        srng.set(si, evs)
                        ev_s.append((ki, kt, evs))
                    for oc in range(DC):
                        ai, awar = arng.next()
                        bank = pav[ai][:, :]
                        mm = None
                        n = len(ev_s)
                        for j, (ki, kt, evs) in enumerate(ev_s):
                            mm = P.op("tensor", lambda E, b=bank, ki=ki, kt=kt, oc=oc, j=j, n=n: E.matmul(
                                b, vtmb[:, (kt * DC + oc) * 128:(kt * DC + oc) * 128 + 128],
                                sS[:, ki * 128:(ki + 1) * 128],
                                start=(j == 0), stop=(j == n - 1)),
                                after=[evs, ev_v[(oc, kt)]] + ([awar] if j == 0 else []),
                                inc=(j == n - 1))
                        evb = P.op("scalar", lambda E, b=bank, bd=bnd, oc=oc, qi=qi: E.copy(
                            bd[:, oc * 512 + qi * 128:oc * 512 + (qi + 1) * 128], b),
                            after=[mm, prev_ap1[par]], inc=True)
                        arng.set(ai, evb)
                        bnd_tok[oc] = evb
                    sS_war = [bnd_tok[DC - 1]] * 3
                return bnd_tok

            def emit_group_ln(gi_, qg, bnd_tok):
                nonlocal prev_vadd, prev_ap1
                par = gi_ % 2
                bnd = bnd2[par]
                outs = outs2[par]
                w = 128 * len(qg)
                r_toks = [bnd_tok[oc] for oc in range(DC)]
                r_ap = lambda oc, bd=bnd: bd[:, oc * 512:oc * 512 + w]
                dst_ap = lambda oc: rb2[:, oc * 512:oc * 512 + w]
                ap1 = emit_ln_gated(P, C, lnb, w, r_ap, dst_ap, r_toks, prev_vadd,
                                    affine=(lambda oc: lnw[:, oc:oc + 1],
                                            lambda oc: lnbw[:, oc:oc + 1]))
                prev_ap1[par] = ap1[-1]
                t0g = qg[0] * 128
                vadds = [None] * DC
                xrng = Ring(2)
                for oc in range(DC):
                    xi, xwar = xrng.next()
                    xd = P.dma("sync", rx[:, xi * 512:xi * 512 + w],
                               dr["x2T"][oc * 128:(oc + 1) * 128, t0g:t0g + w],
                               after=[xwar])
                    va = P.op("vector", lambda E, oc=oc, xi=xi, w=w: E.tensor_add(
                        rb2[:, oc * 512:oc * 512 + w],
                        rb2[:, oc * 512:oc * 512 + w],
                        rx[:, xi * 512:xi * 512 + w]),
                        after=[ap1[oc], xd], inc=True)
                    xrng.set(xi, va)
                    vadds[oc] = va
                prev_vadd = vadds
                r_ap2 = lambda oc: rb2[:, oc * 512:oc * 512 + w]
                dst_ap2 = lambda oc, o=outs: o[:, oc * 512:oc * 512 + w]
                ap2 = emit_ln_gated(P, C, lnb, w, r_ap2, dst_ap2, vadds, prev_store[par])
                for oc in range(DC):
                    st_ = P.dma("sync", dr["x3T"][oc * 128:(oc + 1) * 128, t0g:t0g + w],
                                outs[:, oc * 512:oc * 512 + w], after=[ap2[oc]])
                    prev_store[par][oc] = st_

            pend = None
            for gi_, qg in enumerate(qt_groups):
                bnd_tok = emit_attn_group(gi_, qg)
                if pend is not None:
                    emit_group_ln(*pend)
                pend = (gi_, qg, bnd_tok)
            emit_group_ln(*pend)


def emit_stage_d(nc, P, C, dr):
    TH = 512
    with ExitStack() as stk:
        sb = lambda nm, sh, dt: stk.enter_context(nc.sbuf_tensor(nm, sh, dt)).ap()
        ps = lambda nm, sh, dt: stk.enter_context(nc.psum_tensor(nm, sh, dt)).ap()
        w2b = sb("d_w2b", [128, FC * 1024], BF16)
        b1s = sb("d_b1s", [128, 32], F32)
        b2s = sb("d_b2s", [128, 8], F32)
        x3b2 = [sb("d_x3ba", [128, DC * TH], BF16), sb("d_x3bb", [128, DC * TH], BF16)]
        hT = sb("d_hT", [128, FC * TH], BF16)
        chunks = tiles_of(TOWN, TH)
        w1r = [sb("d_w1r0", [128, DC * 128], BF16), sb("d_w1r1", [128, DC * 128], BF16)]
        rbuf2 = [sb("d_rbufa", [128, DC * 512], F32), sb("d_rbufb", [128, DC * 512], F32)]
        outs2 = [sb("d_outsa", [128, DC * 512], F32), sb("d_outsb", [128, DC * 512], F32)]
        rx = sb("d_rx", [128, 4 * 512], F32)
        lnb = make_ln_bufs(nc, stk, "d_ln")
        pd1 = [ps("d_pd10", [128, 512], F32), ps("d_pd11", [128, 512], F32)]
        pd2 = [ps("d_pd20", [128, 512], F32), ps("d_pd21", [128, 512], F32)]
        tok_w2 = [None] * FC
        tok_b = [P.dma("sync", b1s[:, :], dr["b1fT"][:, :]),
                 P.dma("sync", b2s[:, :], dr["b2fT"][:, :])]
        w1rng = Ring(2)
        p1rng = Ring(2)
        p2rng = Ring(2)
        prev_store = [[None] * DC, [None] * DC]
        prev_apply = [[None] * DC, [None] * DC]
        xrng = Ring(4)
        prev_third = [None, None]
        gti = 0
        for ci, (h0, THc) in enumerate(chunks):
            x3b = x3b2[ci % 2]
            tok_x3 = [P.dma("gpsimd", x3b[:, kc * TH:kc * TH + THc],
                            dr["x3T"][kc * 128:(kc + 1) * 128, OFF + h0:OFF + h0 + THc],
                            after=[prev_third[ci % 2]]) for kc in range(DC)]
            ev_h_last = None
            for oc in range(FC):
                wi, wwar = w1rng.next()
                dtok = P.dma("gpsimd",
                             w1r[wi].rearrange("p (kc n) -> p kc n", kc=DC),
                             dr["w1f"][:, oc * 128:(oc + 1) * 128]
                             .rearrange("(kc p) n -> p kc n", p=128),
                             after=[wwar])
                if ci == 0:
                    tok_w2[oc] = P.dma("gpsimd",
                                       w2b[:, oc * 1024:(oc + 1) * 1024],
                                       dr["w2f"][oc * 128:(oc + 1) * 128, :])
                last_mm = None
                for (t0, w) in tiles_of(THc, 512):
                    pi, pwar = p1rng.next()
                    bank = pd1[pi][:, :w]
                    mm = None
                    for kc in range(DC):
                        mm = P.op("tensor", lambda E, b=bank, x3b=x3b, wi=wi, kc=kc, t0=t0, w=w: E.matmul(
                            b, w1r[wi][:, kc * 128:(kc + 1) * 128],
                            x3b[:, kc * TH + t0:kc * TH + t0 + w],
                            start=(kc == 0), stop=(kc == DC - 1)),
                            after=[dtok, tok_x3[kc]] + ([pwar] if kc == 0 else []),
                            inc=(kc == DC - 1))
                    ev = P.op("scalar", lambda E, b=bank, oc=oc, t0=t0, w=w: E.activation(
                        out=hT[:, oc * TH + t0:oc * TH + t0 + w], in_=b,
                        func=AF.Relu, bias=b1s[:, oc:oc + 1], scale=1.0),
                        after=[mm, tok_b[0]], inc=True)
                    p1rng.set(pi, ev)
                    last_mm = mm
                    ev_h_last = ev
                w1rng.set(wi, last_mm)
            for (t0, w) in tiles_of(THc, 512):
                par = gti % 2
                gti += 1
                rbuf = rbuf2[par]
                outs = outs2[par]
                radd = [None] * DC
                for oc in range(DC):
                    pi, pwar = p2rng.next()
                    bank = pd2[pi][:, :w]
                    mm = None
                    for kc in range(FC):
                        mm = P.op("tensor", lambda E, b=bank, kc=kc, oc=oc, t0=t0, w=w: E.matmul(
                            b, w2b[:, kc * 1024 + oc * 128:kc * 1024 + (oc + 1) * 128],
                            hT[:, kc * TH + t0:kc * TH + t0 + w],
                            start=(kc == 0), stop=(kc == FC - 1)),
                            after=[tok_w2[kc], ev_h_last] + ([pwar] if kc == 0 else []),
                            inc=(kc == FC - 1))
                    ev = P.op("scalar", lambda E, b=bank, r=rbuf, oc=oc, w=w: E.activation(
                        out=r[:, oc * 512:oc * 512 + w], in_=b, func=AF.Identity,
                        bias=b2s[:, oc:oc + 1], scale=1.0),
                        after=[mm, tok_b[1], prev_apply[par][oc]], inc=True)
                    p2rng.set(pi, ev)
                    xi, xwar = xrng.next()
                    xd = P.dma("sync", rx[:, xi * 512:xi * 512 + w],
                               dr["x3T"][oc * 128:(oc + 1) * 128,
                                         OFF + h0 + t0:OFF + h0 + t0 + w],
                               after=[xwar])
                    ra = P.op("vector", lambda E, r=rbuf, oc=oc, xi=xi, w=w: E.tensor_add(
                        r[:, oc * 512:oc * 512 + w],
                        r[:, oc * 512:oc * 512 + w],
                        rx[:, xi * 512:xi * 512 + w]),
                        after=[ev, xd], inc=True)
                    xrng.set(xi, ra)
                    radd[oc] = ra
                r_ap = lambda oc, r=rbuf: r[:, oc * 512:oc * 512 + w]
                dst_ap = lambda oc, o=outs: o[:, oc * 512:oc * 512 + w]
                ap_toks = emit_ln_gated(P, C, lnb, w, r_ap, dst_ap, radd,
                                        prev_store[par])
                for oc in range(DC):
                    st_ = P.dma("sync", dr["yT"][oc * 128:(oc + 1) * 128, h0 + t0:h0 + t0 + w],
                                outs[:, oc * 512:oc * 512 + w], after=[ap_toks[oc]])
                    prev_store[par][oc] = st_
                    prev_apply[par][oc] = ap_toks[oc]
            prev_third[ci % 2] = ev_h_last
            ev_h_last = None


def _dram_inputs_a(nc):
    dr = {}
    dr["xT"] = nc.dram_tensor("xT", [D, TOWN], F32, kind="ExternalInput").ap()
    dr["ctxT"] = nc.dram_tensor("ctxT", [CTXD, M], F32, kind="ExternalInput").ap()
    dr["wq"] = nc.dram_tensor("wq", [D, D], F32, kind="ExternalInput").ap()
    dr["wk"] = nc.dram_tensor("wk", [CTXD, D], F32, kind="ExternalInput").ap()
    dr["wv"] = nc.dram_tensor("wv", [CTXD, D], F32, kind="ExternalInput").ap()
    dr["wo"] = nc.dram_tensor("wo", [D, D], F32, kind="ExternalInput").ap()
    dr["boutT"] = nc.dram_tensor("boutT", [128, 8], F32, kind="ExternalInput").ap()
    return dr


def _dram_inputs_b(nc):
    dr = {}
    dr["wr"] = nc.dram_tensor("wr", [D, 8], F32, kind="ExternalInput").ap()
    dr["w1moe"] = nc.dram_tensor("w1moe", [NE, D, FF], F32, kind="ExternalInput").ap()
    dr["b1moeT"] = nc.dram_tensor("b1moeT", [NE, 128, 32], F32, kind="ExternalInput").ap()
    dr["w2moe"] = nc.dram_tensor("w2moe", [NE, FF, D], F32, kind="ExternalInput").ap()
    dr["b2moeT"] = nc.dram_tensor("b2moeT", [NE, 128, 8], F32, kind="ExternalInput").ap()
    return dr


def _dram_inputs_c(nc):
    dr = {}
    dr["wqkv"] = nc.dram_tensor("wqkv", [D, 3 * D], F32, kind="ExternalInput").ap()
    dr["bqkvT"] = nc.dram_tensor("bqkvT", [128, 24], F32, kind="ExternalInput").ap()
    dr["lnwT"] = nc.dram_tensor("lnwT", [128, 8], F32, kind="ExternalInput").ap()
    dr["lnbT"] = nc.dram_tensor("lnbT", [128, 8], F32, kind="ExternalInput").ap()
    dr["mskL"] = nc.dram_tensor("mskL", [128, 128], F32, kind="ExternalInput").ap()
    dr["mskU"] = nc.dram_tensor("mskU", [128, 128], F32, kind="ExternalInput").ap()
    return dr


def _dram_inputs_d(nc):
    dr = {}
    dr["w1f"] = nc.dram_tensor("w1f", [D, FF], F32, kind="ExternalInput").ap()
    dr["b1fT"] = nc.dram_tensor("b1fT", [128, 32], F32, kind="ExternalInput").ap()
    dr["w2f"] = nc.dram_tensor("w2f", [FF, D], F32, kind="ExternalInput").ap()
    dr["b2fT"] = nc.dram_tensor("b2fT", [128, 8], F32, kind="ExternalInput").ap()
    return dr


def _dram_consts(nc):
    dr = {}
    dr["identb"] = nc.dram_tensor("identb", [128, 128], BF16, kind="ExternalInput").ap()
    dr["identf"] = nc.dram_tensor("identf", [128, 128], F32, kind="ExternalInput").ap()
    dr["ones_col_b"] = nc.dram_tensor("ones_col_b", [128, 1], BF16, kind="ExternalInput").ap()
    dr["ones_col_f"] = nc.dram_tensor("ones_col_f", [128, 1], F32, kind="ExternalInput").ap()
    dr["ones_row"] = nc.dram_tensor("ones_row", [1, 128], F32, kind="ExternalInput").ap()
    dr["triuS"] = nc.dram_tensor("triuS", [128, 128], F32, kind="ExternalInput").ap()
    dr["erowT"] = nc.dram_tensor("erowT", [128, 144], F32, kind="ExternalInput").ap()
    dr["epsc"] = nc.dram_tensor("epsc", [1, 1], F32, kind="ExternalInput").ap()
    return dr


def build_stage(stage, debug=False):
    nc = bass.Bass("TRN2", target_bir_lowering=False)
    with ExitStack() as stack:
        P = Prog(nc, stack)
        dr = _dram_consts(nc)
        if stage == "a":
            dr.update(_dram_inputs_a(nc))
            dr["x1T"] = nc.dram_tensor("x1T", [D, T], F32, kind="ExternalOutput").ap()
            if debug:
                dr["dbg"] = True
                dr["dbg_qt"] = nc.dram_tensor("dbg_qt", [128, DC * T], BF16, kind="ExternalOutput").ap()
                dr["dbg_ao"] = nc.dram_tensor("dbg_ao", [128, DC * T], BF16, kind="ExternalOutput").ap()
                dr["dbg_kt"] = nc.dram_tensor("dbg_kt", [128, M], BF16, kind="ExternalOutput").ap()
                dr["dbg_vt"] = nc.dram_tensor("dbg_vt", [128, M], BF16, kind="ExternalOutput").ap()
                dr["dbg_p"] = nc.dram_tensor("dbg_p", [128, 16384], BF16, kind="ExternalOutput").ap()
                dr["dbg_invB"] = nc.dram_tensor("dbg_invB", [128, 512], F32, kind="ExternalOutput").ap()
                dr["dbg_un0"] = nc.dram_tensor("dbg_un0", [128, 16384], BF16, kind="ExternalOutput").ap()
                dr["dbg_inv1"] = nc.dram_tensor("dbg_inv1", [1, 512], F32, kind="ExternalOutput").ap()
                dr["dbg_r"] = nc.dram_tensor("dbg_r", [128, DC * 512], F32, kind="ExternalOutput").ap()
                dr["dbg_st"] = nc.dram_tensor("dbg_st", [1, 6 * 512], F32, kind="ExternalOutput").ap()
        elif stage == "b":
            dr.update(_dram_inputs_b(nc))
            dr["x1T"] = nc.dram_tensor("x1T", [D, T], F32, kind="ExternalInput").ap()
            dr["x2T"] = nc.dram_tensor("x2T", [D, T], F32, kind="ExternalOutput").ap()
            if not debug:
                dr["xg"] = nc.dram_tensor("xg", [NE * CAP, D], BF16).ap()
                dr["og"] = nc.dram_tensor("og", [NE * CAP, D], BF16).ap()
            if debug:
                dr["dbg"] = True
                dr["dbg_lg"] = nc.dram_tensor("dbg_lg", [8, T], F32, kind="ExternalOutput").ap()
                for nm, sh in [("dbg_oh", [128, T // 128 * 8]), ("dbg_pm", [128, T // 128]),
                               ("dbg_rank", [128, T // 128]), ("dbg_eid", [128, T // 128])]:
                    dr[nm] = nc.dram_tensor(nm, sh, F32, kind="ExternalOutput").ap()
                dr["dbg_idx"] = nc.dram_tensor("dbg_idx", [128, T // 128], I32, kind="ExternalOutput").ap()
                dr["dbg_cum"] = nc.dram_tensor("dbg_cum", [128, T // 128 * 8], F32, kind="ExternalOutput").ap()
                dr["dbg_tot"] = nc.dram_tensor("dbg_tot", [1, T // 128 * 8], F32, kind="ExternalOutput").ap()
                dr["dbg_base"] = nc.dram_tensor("dbg_base", [1, T // 128 * 8], F32, kind="ExternalOutput").ap()
                dr["xg"] = nc.dram_tensor("xg", [NE * CAP, D], BF16, kind="ExternalOutput").ap()
                dr["og"] = nc.dram_tensor("og", [NE * CAP, D], BF16, kind="ExternalOutput").ap()
        elif stage == "c":
            dr.update(_dram_inputs_c(nc))
            dr["x2T"] = nc.dram_tensor("x2T", [D, T], F32, kind="ExternalInput").ap()
            dr["x3T"] = nc.dram_tensor("x3T", [D, T], F32, kind="ExternalOutput").ap()
            dr["hmask"] = nc.dram_tensor("hmask", [128, 1], F32, kind="ExternalInput").ap()
        elif stage == "d":
            dr.update(_dram_inputs_d(nc))
            dr["x3T"] = nc.dram_tensor("x3T", [D, T], F32, kind="ExternalInput").ap()
            dr["yT"] = nc.dram_tensor("yT", [D, TOWN], F32, kind="ExternalOutput").ap()
        with ExitStack() as cstk:
            Cs = emit_consts(nc, P, cstk, dr)
            if stage == "a":
                emit_stage_a(nc, P, Cs, dr)
            elif stage == "b":
                emit_stage_b(nc, P, Cs, dr)
            elif stage == "c":
                emit_stage_c(nc, P, Cs, dr)
            elif stage == "d":
                emit_stage_d(nc, P, Cs, dr)
            P.barrier()
            P.emit()
    return nc


def build_stage_a(debug=False):
    return build_stage("a", debug)


def build_fused():
    nc = bass.Bass("TRN2", target_bir_lowering=False)
    with ExitStack() as stack:
        P = Prog(nc, stack)
        dr = _dram_consts(nc)
        dr.update(_dram_inputs_a(nc))
        dr.update(_dram_inputs_b(nc))
        dr.update(_dram_inputs_c(nc))
        dr.update(_dram_inputs_d(nc))
        dr["x1T"] = nc.dram_tensor("x1T", [D, T], F32).ap()
        dr["x2T"] = nc.dram_tensor("x2T", [D, T], F32).ap()
        dr["x3T"] = nc.dram_tensor("x3T", [D, T], F32).ap()
        dr["yT"] = nc.dram_tensor("yT", [D, TOWN], F32, kind="ExternalOutput").ap()
        dr["xg"] = nc.dram_tensor("xg", [NE * CAP, D], BF16).ap()
        dr["og"] = nc.dram_tensor("og", [NE * CAP, D], BF16).ap()
        dr["x2tail"] = nc.dram_tensor("x2tail", [128, DC * OFF], F32)
        dr["x2gat"] = nc.dram_tensor("x2gat", [8 * 128, DC * OFF], F32,
                                     addr_space="Shared")
        dr["halo_idx"] = nc.dram_tensor("halo_idx", [128, 1], I32, kind="ExternalInput").ap()
        dr["hmask"] = nc.dram_tensor("hmask", [128, 1], F32, kind="ExternalInput").ap()
        with ExitStack() as cstk:
            Cs = emit_consts(nc, P, cstk, dr)
            emit_stage_a(nc, P, Cs, dr)
            P.barrier()
            P.emit()
            emit_stage_b(nc, P, Cs, dr,
                         tail_hook=lambda: emit_halo_issue(nc, P, Cs, dr,
                                                           P.halo_sbuf))
            emit_halo_exchange(nc, P, Cs, dr, P.halo_sbuf)
            P.barrier()
            P.emit()
            emit_stage_c(nc, P, Cs, dr)
            P.barrier()
            P.emit()
            emit_stage_d(nc, P, Cs, dr)
            P.barrier()
            P.emit()
    return nc


def host_consts():
    return {
        "identb": np.eye(128, dtype=bfdt),
        "identf": np.eye(128, dtype=np.float32),
        "ones_col_b": np.ones((128, 1), dtype=bfdt),
        "ones_col_f": np.ones((128, 1), dtype=np.float32),
        "ones_row": np.ones((1, 128), dtype=np.float32),
        "triuS": np.triu(np.ones((128, 128), np.float32), 1),
        "erowT": np.tile(np.arange(8, dtype=np.float32), (128, T // 128)),
        "epsc": np.full((1, 1), EPS, np.float32),
    }


def core_window(c):
    b, h = c // 2, c % 2
    start = 0 if h == 0 else 4096 - T
    own0 = 0 if h == 0 else T - OWN
    return b, start, own0


def slice_stream(full, c):
    b, h = c // 2, c % 2
    return np.ascontiguousarray(full[b, h * OWN:(h + 1) * OWN, :].T.astype(np.float32))


def assemble_stream(parts):
    out = np.empty((4, 4096, D), np.float32)
    for c in range(8):
        b, h = c // 2, c % 2
        out[b, h * OWN:(h + 1) * OWN, :] = parts[c].T
    return out


def halo_inputs(c):
    h = c % 2
    if h == 1:
        idx = ((c - 1) * 128 + np.arange(128)[:, None]).astype(np.int32)
        hm = np.ones((128, 1), np.float32)
    else:
        idx = np.full((128, 1), 10 ** 6, np.int32)
        hm = np.zeros((128, 1), np.float32)
    return idx, hm


def _bT(v, chunks):
    return np.ascontiguousarray(np.asarray(v, np.float32).reshape(chunks, 128).T)


def stage_weight_maps(inputs):
    f32 = np.float32
    a = {
        "ctxT": np.ascontiguousarray(
            np.asarray(inputs["L0_attn_context"], f32).reshape(M, CTXD).T),
        "wq": np.asarray(inputs["L0_attn_Wq"], f32),
        "wk": np.asarray(inputs["L0_attn_Wk"], f32),
        "wv": np.asarray(inputs["L0_attn_Wv"], f32),
        "wo": np.asarray(inputs["L0_attn_Wout"], f32),
        "boutT": _bT(inputs["L0_attn_bout"], 8),
    }
    b = {
        "wr": np.asarray(inputs["L0_ffn_Wr"], f32),
        "w1moe": np.asarray(inputs["L0_ffn_W1"], f32),
        "b1moeT": np.ascontiguousarray(
            np.asarray(inputs["L0_ffn_b1"], f32).reshape(NE, 32, 128).transpose(0, 2, 1)),
        "w2moe": np.asarray(inputs["L0_ffn_W2"], f32),
        "b2moeT": np.ascontiguousarray(
            np.asarray(inputs["L0_ffn_b2"], f32).reshape(NE, 8, 128).transpose(0, 2, 1)),
    }
    ones = np.ones((128, 128), f32)
    cc = {
        "wqkv": np.asarray(inputs["L1_attn_W_qkv"], f32),
        "bqkvT": _bT(inputs["L1_attn_b_qkv"], 24),
        "lnwT": _bT(inputs["L1_attn_ln_w"], 8),
        "lnbT": _bT(inputs["L1_attn_ln_b"], 8),
        "mskL": np.ascontiguousarray(np.tril(ones, -1) * SCALE_C),
        "mskU": np.ascontiguousarray(np.triu(ones) * SCALE_C),
    }
    d = {
        "w1f": np.asarray(inputs["L1_ffn_W1"], f32),
        "b1fT": _bT(inputs["L1_ffn_b1"], 32),
        "w2f": np.asarray(inputs["L1_ffn_W2"], f32),
        "b2fT": _bT(inputs["L1_ffn_b2"], 8),
    }
    return a, b, cc, d


def stage_a_in_maps(inputs):
    cst = host_consts()
    a, _, _, _ = stage_weight_maps(inputs)
    base = dict(cst); base.update(a)
    x = np.asarray(inputs["L0_attn_x"], np.float32)
    maps = []
    for c in range(8):
        m = dict(base)
        m["xT"] = slice_stream(x, c)
        maps.append(m)
    return maps


def _launch(stage, base, stream_key, stream_full):
    nc = build_stage(stage)
    maps = []
    for c in range(8):
        m = dict(base)
        m[stream_key] = slice_stream(stream_full, c)
        maps.append(m)
    res = run_bass_kernel_spmd(nc, maps, core_ids=list(range(8)))
    return res


def kernel(**inputs):
    cst = host_consts()
    wa, wb, wc, wd = stage_weight_maps(inputs)
    x = np.asarray(inputs["L0_attn_x"], np.float32)
    nc = build_fused()
    base = dict(cst)
    for d_ in (wa, wb, wc, wd):
        base.update(d_)
    maps = []
    for c in range(8):
        m = dict(base)
        m["xT"] = slice_stream(x, c)
        m["halo_idx"], m["hmask"] = halo_inputs(c)
        maps.append(m)
    res = run_bass_kernel_spmd(nc, maps, core_ids=list(range(8)))
    return assemble_stream([res.results[c]["yT"] for c in range(8)])


def kernel_unfused(**inputs):
    cst = host_consts()
    wa, wb, wc, wd = stage_weight_maps(inputs)
    x = np.asarray(inputs["L0_attn_x"], np.float32)
    base_a = dict(cst); base_a.update(wa)
    res = _launch("a", base_a, "xT", x)
    x1 = assemble_stream([res.results[c]["x1T"] for c in range(8)])
    base_b = dict(cst); base_b.update(wb)
    res = _launch("b", base_b, "x1T", x1)
    x2 = assemble_stream([res.results[c]["x2T"] for c in range(8)])
    base_c = dict(cst); base_c.update(wc)
    res = _launch("c", base_c, "x2T", x2)
    x3 = assemble_stream([res.results[c]["x3T"] for c in range(8)])
    base_d = dict(cst); base_d.update(wd)
    res = _launch("d", base_d, "x3T", x3)
    return assemble_stream([res.results[c]["yT"] for c in range(8)])

